# version 28
# baseline (speedup 1.0000x reference)
import numpy as np
import ml_dtypes
from contextlib import ExitStack
import concourse.bass as bass
import concourse.mybir as mybir
from concourse.bass_utils import run_bass_kernel_spmd

F32 = mybir.dt.float32
BF16 = mybir.dt.bfloat16
AF = mybir.ActivationFunctionType
ALU = mybir.AluOpType
AXX = mybir.AxisListType.X

HID = 2048
KC = HID // 128
I = 4096
NF = 128
CONV = I + 2 * NF
DTF = 64
PD = 64
Q = 256
TM = 1024
HALO = 256
EPS = 1e-5
OCX = 32
OCB = 32
OCC = 33
NOC = 34
JC = 11
E = 8
ENGINES = ("tensor", "vector", "scalar", "gpsimd", "sync")


AUTOSEM = {"tensor": "t", "vector": "v", "scalar": "s", "gpsimd": "g", "sync": None}


class B:

    def __init__(self):
        self.ops = {e: [] for e in ENGINES}
        self.cnt = {}

    def op(self, eng, fn, incs=()):
        if incs and incs[0][1] == 16:
            use = tuple(incs[:1])
        else:
            a = AUTOSEM[eng]
            use = ((a, 1),) if a else ()
        for sname, n in use:
            self.cnt[sname] = self.cnt.get(sname, 0) + n
        self.ops[eng].append((fn, use))

    def wait(self, eng, sem, val=None):
        vv = self.cnt.get(sem, 0) if val is None else val
        if vv <= 0:
            return
        self.ops[eng].append((("wait", sem, vv), ()))

    def mark(self, sem):
        return self.cnt.get(sem, 0)

    def retro(self, eng, incs):
        pass


class Banks:

    def __init__(self, b, aps):
        self.b = b
        self.aps = aps
        self.rel = [None] * len(aps)
        self.i = 0

    def acquire(self):
        idx = self.i % len(self.aps)
        self.i += 1
        if self.rel[idx] is not None:
            sem, mk = self.rel[idx]
            self.b.wait("tensor", sem, mk)
        return idx, self.aps[idx]

    def release(self, idx, eng="vector"):
        self.rel[idx] = (AUTOSEM[eng], self.b.mark(AUTOSEM[eng]))


def build_nc():
    nc = bass.Bass("TRN2", target_bir_lowering=False, debug=False,
                   detect_race_conditions=False)
    dv = {}
    for n, shp, d in [
        ("xm", [TM, HID], F32), ("xh", [HALO, HID], F32), ("hsc", [128, 1], F32),
        ("wT", [HID, 8512], F32), ("woT", [I, HID], F32),
        ("gupT", [E, HID, 2816], F32), ("downT", [E, 1408, HID], F32),
        ("routT", [HID, E], F32),
        ("convw", [128, NOC * 4], F32), ("convb", [128, NOC], F32),
        ("nwpc", [128, OCX], F32), ("acol", [DTF, 1], F32),
        ("dtb", [DTF, 1], F32), ("drow", [1, DTF], F32),
    ]:
        dv[n] = nc.dram_tensor(n, shp, d, kind="ExternalInput").ap()
    dv["out"] = nc.dram_tensor("out", [TM, HID], F32, kind="ExternalOutput").ap()

    es = ExitStack()
    sbt = lambda n, shp, d: es.enter_context(nc.sbuf_tensor(n, shp, d)).ap()
    pst = lambda n, shp, d: es.enter_context(nc.psum_tensor(n, shp, d)).ap()

    xn = sbt("xn", [128, KC, TM], BF16)
    bigaflat = sbt("biga", [128, NOC * 515], BF16)
    biga = bigaflat.rearrange("p (a b) -> p a b", a=NOC)
    ypflat = sbt("ypair", [128, OCX * 256], BF16)
    ypair = ypflat.rearrange("p (a b) -> p a b", a=OCX)
    sgj = sbt("sgj", [128, 2, 2, 512], BF16)
    oaflat = sbt("outacc", [128, KC * TM], BF16)
    outacc = oaflat.rearrange("p (a b) -> p a b", a=KC)
    wbuf = sbt("wbuf", [128, 4, 2048], BF16)
    sst = sbt("sst", [128, I], BF16)
    xtm2 = sbt("xtm2", [128, 2, I], BF16)
    xscr = sbt("xscr", [128, I], BF16)
    ytm = sbt("ytm", [128, I], BF16)
    lead3 = sbt("lead3", [128, NOC, 3], BF16)
    convo = sbt("convo", [128, 4, Q], BF16)
    bfm = sbt("bfm", [128, Q], BF16)
    cfm = sbt("cfm", [128, Q], BF16)
    btm = sbt("btm", [128, 2, 128], BF16)
    cbt = sbt("cbt", [128, 3, 128], BF16)
    md8 = sbt("md8", [128, 2, 8, 128], BF16)
    dtmB = sbt("dtm", [DTF, TM], F32)
    dthB = None
    csb = sbt("csb", [DTF, Q], F32)
    cst = sbt("cst", [128, 2, DTF], F32)
    ncst = sbt("ncst", [128, 2, DTF], F32)
    dtt = sbt("dtt", [128, 2, DTF], F32)
    rfT = sbt("rfT", [128, 2, DTF], F32)
    wsT = sbt("wsT", [128, 2, DTF], F32)
    woT_ = sbt("woT_", [128, DTF], F32)
    roT_ = sbt("roT_", [128, DTF], F32)
    t64a = sbt("t64a", [1, DTF], F32)
    dbct = sbt("dbct", [128, DTF], F32)
    cacc = sbt("cacc", [128, 2, 256], F32)
    convw = sbt("convw_s", [128, NOC * 4], F32)
    convb = sbt("convb_s", [128, NOC], F32)
    nwpc = sbt("nwpc_s", [128, OCX], F32)
    acol = sbt("acol_s", [DTF, 1], F32)
    dtbs = sbt("dtb_s", [DTF, 1], F32)
    drow = sbt("drow_s", [1, DTF], F32)
    hscs = sbt("hsc_s", [128, 1], F32)
    rw8 = sbt("rw8", [128, KC, E], BF16)
    idb = sbt("idb", [128, 128], BF16)
    idf = sbt("idf", [128, 128], F32)
    trilb = sbt("trilb", [128, 128], BF16)
    ones64 = sbt("ones64", [64, 128], F32)
    onescol = sbt("onescol", [128, 1], BF16)
    comb = sbt("comb", [128, E * 8], F32)
    lgt = sbt("lgt", [128, 16], F32)
    m12 = sbt("m12", [128, 4], F32)
    lncol = sbt("lncol", [128, 4], F32)
    scrv = sbt("scrv", [128, 512], F32)
    rr = scrv[0:1, 0:512]
    g2t = scrv.bitcast(BF16).rearrange("p (a b) -> p a b", a=2)
    sgt = cacc.rearrange("p a b -> p (a b)").bitcast(BF16).rearrange("p (a b) -> p a b", a=2)
    combT = ytm[0:E, 0:TM]
    csb2 = scrv[0:DTF, 0:Q]

    dthB = dtmB[:, 0:HALO]
    lnscr = oaflat[:, 8192:12288].bitcast(F32)
    xout5 = [bigaflat[:, 8192 + i * 4096: 8192 + (i + 1) * 4096].bitcast(F32)
             for i in range(2)]
    xld0 = [oaflat[:, i * 4096:(i + 1) * 4096].bitcast(F32) for i in range(2)]
    xld5 = [ypflat[:, i * 4096:(i + 1) * 4096].bitcast(F32) for i in range(2)]
    xnh = ypflat[:, 0:KC * HALO].rearrange("p (k t) -> p k t", k=KC)
    amoe = bigaflat[:, 0:JC * 1024].rearrange("p (a b) -> p a b", a=JC)

    pmm = [pst(f"pmm{i}", [128, 512], F32) for i in range(4)]
    ptrb = [pst(f"ptrb{i}", [128, 1024], BF16) for i in range(2)]
    psm = [pst(f"psm{i}", [128, 512], F32) for i in range(2)]

    sems = {}
    for sname in ("x0", "x1", "w", "w0", "w1", "w2", "w3", "o", "t", "v", "s", "g", "wf", "pf"):
        sems[sname] = es.enter_context(nc.semaphore(sname))

    b = B()
    MM = Banks(b, pmm)
    TRB = Banks(b, ptrb)
    SML = Banks(b, [psm[0]])
    SMH = Banks(b, [psm[1]])

    def t(fn, incs=()):
        b.op("tensor", fn, incs)

    def v(fn, incs=()):
        b.op("vector", fn, incs)

    def s(fn, incs=()):
        b.op("scalar", fn, incs)

    def g(fn, incs=()):
        b.op("gpsimd", fn, incs)

    def sy(fn, incs=()):
        b.op("sync", fn, incs)


    def vred(fn):
        b.wait("vector", "v", b.mark("v"))
        v(fn)
        b.wait("vector", "v", b.mark("v"))

    def hbc(src2d, hs):
        return src2d.unsqueeze(-1).broadcast_to([128, hs, PD])

    wstate = {"iss": 0, "marks": {}, "relq": {}}

    NWS = 4

    def wload(src_fn):
        i = wstate["iss"]
        if i >= NWS:
            rel = wstate["relq"].get(i - NWS)
            if rel is not None:
                b.wait("gpsimd", rel[0], rel[1])
        ap = src_fn()
        kcs, oc = ap.shape[1], ap.shape[2]
        dst = wbuf[:, i % NWS, 0:kcs * oc].rearrange("p (k o) -> p k o", k=kcs)
        wsem = f"w{i % NWS}"
        b.op("gpsimd", lambda EE, d=dst, a=ap: EE["gpsimd"].dma_start(out=d, in_=a),
             [(wsem, 16)])
        wstate["marks"][i] = (wsem, b.mark(wsem))
        wstate["iss"] = i + 1
        return i

    def wrel(i):
        wstate["relq"][i] = ("t", b.mark("t"))

    def wslice(i, kcs, ocols):
        return wbuf[:, i % NWS, 0:kcs * ocols].rearrange("p (k o) -> p k o", k=kcs)

    for name, dst in (("convw", convw), ("convb", convb), ("nwpc", nwpc),
                      ("acol", acol), ("dtb", dtbs), ("drow", drow), ("hsc", hscs)):
        b.op("gpsimd", lambda EE, d=dst, n=name: EE["gpsimd"].dma_start(out=d[:, :], in_=dv[n][:, :]),
             [("w", 16)])
    g(lambda EE: EE["gpsimd"].dma_start(
        out=rw8[:, :, :], in_=dv["routT"].rearrange("(c i) o -> i c o", i=128)),
      [("w", 16)])
    g(lambda EE: EE["gpsimd"].memset(idb[:, :], 0.0))
    g(lambda EE: EE["gpsimd"].memset(idf[:, :], 0.0))
    g(lambda EE: EE["gpsimd"].memset(trilb[:, :], 0.0))
    g(lambda EE: EE["gpsimd"].memset(ones64[:, :], 1.0))
    g(lambda EE: EE["gpsimd"].memset(onescol[:, :], 1.0), [("g", 1)])
    b.wait("gpsimd", "g", b.mark("g"))
    g(lambda EE: EE["gpsimd"].affine_select(out=idb[:, :], in_=idb[:, :], compare_op=ALU.not_equal,
                                            fill=1.0, base=0, pattern=[[-1, 128]], channel_multiplier=1))
    g(lambda EE: EE["gpsimd"].affine_select(out=idf[:, :], in_=idf[:, :], compare_op=ALU.not_equal,
                                            fill=1.0, base=0, pattern=[[-1, 128]], channel_multiplier=1))
    g(lambda EE: EE["gpsimd"].affine_select(out=trilb[:, :], in_=trilb[:, :], compare_op=ALU.is_gt,
                                            fill=1.0, base=0, pattern=[[-1, 128]], channel_multiplier=1),
      [("g", 1)])
    b.wait("tensor", "g", b.mark("g"))
    b.wait("tensor", "w", b.mark("w"))
    b.wait("vector", "g", b.mark("g"))
    b.wait("vector", "w", b.mark("w"))
    b.wait("scalar", "w", b.mark("w"))

    bi, bk = SML.acquire()
    t(lambda EE, bk=bk: EE["tensor"].matmul(bk[:, 0:DTF], ones64[0:1, :], drow[0:1, :],
                                            start=True, stop=True), [("t", 1)])
    b.wait("vector", "t", b.mark("t"))
    v(lambda EE, bk=bk: EE["vector"].tensor_copy(dbct[:, :], bk[:, 0:DTF]), [("pf", 1)])
    SML.release(bi)

    half_free_mark = [0, 0]
    stg_guard = [0]

    def ln_tile(ti, src_ap, xn_dst, tcol):
        hb = ti % 2
        b.wait("sync", "v", half_free_mark[hb])
        half = xld0[hb]
        xs_ = f"x{hb}"
        sy(lambda EE, h=half, a=src_ap, : EE["sync"].dma_start(out=h[:, 0:2048], in_=a),
           [(xs_, 16)])
        b.wait("vector", xs_, b.mark(xs_))
        h = half[:, 0:2048]
        vred(lambda EE, h=h: EE["vector"].reduce_sum(lncol[:, 0:1], h, axis=AXX))
        v(lambda EE: EE["vector"].tensor_scalar(lncol[:, 1:2], lncol[:, 0:1], -1.0 / HID,
                                                None, ALU.mult))
        b.wait("vector", "v", b.mark("v"))
        v(lambda EE, h=h: EE["vector"].tensor_scalar(h, h, lncol[:, 1:2], None, ALU.add))
        v(lambda EE, h=h: EE["vector"].tensor_mul(lnscr[:, 0:2048], h, h))
        vred(lambda EE: EE["vector"].reduce_sum(lncol[:, 2:3], lnscr[:, 0:2048], axis=AXX))
        v(lambda EE: EE["vector"].tensor_scalar(lncol[:, 2:3], lncol[:, 2:3], 1.0 / HID, EPS,
                                                ALU.mult, ALU.add), [("v", 1)])
        b.wait("scalar", "v", b.mark("v"))
        s(lambda EE: EE["scalar"].activation(lncol[:, 3:4], lncol[:, 2:3], AF.Sqrt), [("s", 1)])
        b.wait("vector", "s", b.mark("s"))
        vred(lambda EE: EE["vector"].reciprocal(lncol[:, 3:4], lncol[:, 3:4]))
        stg = ytm[:, 0:2048]
        b.wait("vector", "t", stg_guard[0])
        v(lambda EE, h=h, stg=stg: EE["vector"].tensor_scalar(stg, h, lncol[:, 3:4],
                                                              None, ALU.mult), [("v", 1)])
        half_free_mark[hb] = b.mark("v")
        b.wait("tensor", "v", b.mark("v"))
        for grp in range(2):
            tbi, tbk = TRB.acquire()
            for kk in range(8):
                k = grp * 8 + kk
                t(lambda EE, tbk=tbk, kk=kk, k=k, stg=stg: EE["tensor"].transpose(
                    tbk[:, kk * 128:(kk + 1) * 128], stg[:, k * 128:(k + 1) * 128], idb[:, :]),
                  [("t", 1)] if kk == 7 else ())
            b.wait("vector", "t", b.mark("t"))
            v(lambda EE, tbk=tbk, grp=grp, xn_dst=xn_dst, tcol=tcol: EE["vector"].tensor_copy(
                xn_dst[:, grp * 8:(grp + 1) * 8, tcol:tcol + 128],
                tbk[:, 0:1024].rearrange("p (k t2) -> p k t2", k=8)), [("pf", 1), ("v", 1)])
            TRB.release(tbi)
        stg_guard[0] = b.mark("t")

    for ti in range(2):
        ln_tile(8 + ti, dv["xh"][ti * 128:(ti + 1) * 128, :], xnh, ti * 128)

    def inproj_xbc(xsrc, tok0, twid, dt_dst, dt_col0):
        for oc in range(NOC):
            o0 = I + oc * 128
            wi = wload(lambda o0=o0: dv["wT"][:, o0:o0 + 128].rearrange("(c i) o -> i c o", i=128))
            b.wait("tensor", wstate["marks"][wi][0], wstate["marks"][wi][1])
            ws = wslice(wi, KC, 128)
            mi, mb = MM.acquire()
            for k in range(KC):
                incs = [("t", 1)] if k == KC - 1 else []
                t(lambda EE, mb=mb, ws=ws, k=k: EE["tensor"].matmul(
                    mb[:, 0:twid], ws[:, k, 0:128],
                    xsrc[:, k, tok0:tok0 + twid], start=(k == 0), stop=(k == KC - 1)),
                  incs)
            b.wait("vector", "t", b.mark("t"))
            v(lambda EE, mb=mb, oc=oc, twid=twid: EE["vector"].tensor_copy(
                biga[:, oc, 3:3 + twid], mb[:, 0:twid]), [("pf", 1)])
            MM.release(mi)
            wrel(wi)
        wi = wload(lambda: dv["wT"][:, 8448:8512].rearrange("(c i) o -> i c o", i=128))
        b.wait("tensor", wstate["marks"][wi][0], wstate["marks"][wi][1])
        dtw = wslice(wi, KC, DTF)
        mi, mb = MM.acquire()
        for k in range(KC):
            t(lambda EE, mb=mb, k=k, dtw=dtw: EE["tensor"].matmul(
                mb[0:DTF, 0:twid], dtw[:, k, :], xsrc[:, k, tok0:tok0 + twid],
                start=(k == 0), stop=(k == KC - 1)), [("t", 1)] if k == KC - 1 else ())
        wrel(wi)
        b.wait("scalar", "t", b.mark("t"))
        s(lambda EE, mb=mb, twid=twid: EE["scalar"].activation(
            scrv[0:DTF, 0:twid], mb[0:DTF, 0:twid], AF.Exp, bias=dtbs[:, 0:1]))
        s(lambda EE, twid=twid: EE["scalar"].activation(
            dt_dst[:, dt_col0:dt_col0 + twid], scrv[0:DTF, 0:twid], AF.Ln, bias=1.0))
        MM.release(mi, "scalar")
        b.wait("vector", "s", b.mark("s"))

    cacc_state = {"i": 0, "guard": [0, 0], "gguard": 0, "vguard": 0}

    def conv_oc(oc, c0, dst, eng="vector"):
        pr = cacc_state["i"] % 2
        cacc_state["i"] += 1
        ca = cacc[:, pr, :]
        b.wait(eng, "s", cacc_state["guard"][pr])
        if eng == "gpsimd":
            b.wait("gpsimd", "g", cacc_state["gguard"])
        ee = eng
        def emit(fn, incs=()):
            b.op(ee, fn, incs)
            if ee == "gpsimd":
                b.wait("gpsimd", "g", b.mark("g"))
        emit(lambda EE, oc=oc, c0=c0, ca=ca: EE[ee].tensor_scalar(
            ca, biga[:, oc, c0:c0 + Q], convw[:, oc * 4:oc * 4 + 1], None, ALU.mult))
        for k in range(1, 4):
            emit(lambda EE, oc=oc, c0=c0, k=k, ca=ca: EE[ee].scalar_tensor_tensor(
                ca, biga[:, oc, c0 + k:c0 + k + Q], convw[:, oc * 4 + k:oc * 4 + k + 1],
                ca, ALU.mult, ALU.add))
        if ee == "gpsimd":
            cacc_state["gguard"] = b.mark("g")
            b.wait("scalar", "g", b.mark("g"))
        else:
            b.wait("scalar", "v", b.mark("v"))
        s(lambda EE, oc=oc, dst=dst, ca=ca: EE["scalar"].activation(
            dst, ca, AF.Silu, bias=convb[:, oc:oc + 1]), [("s", 1)])
        cacc_state["guard"][pr] = b.mark("s")

    def fm_to_tm(src, dsts):
        b.wait("tensor", "s", b.mark("s"))
        for qt in range(2):
            tbi, tbk = TRB.acquire()
            t(lambda EE, tbk=tbk, qt=qt, src=src: EE["tensor"].transpose(
                tbk[:, 0:128], src[:, qt * 128:(qt + 1) * 128], idb[:, :]), [("t", 1)])
            b.wait("vector", "t", b.mark("t"))
            v(lambda EE, tbk=tbk, d=dsts[qt]: EE["vector"].tensor_copy(d, tbk[:, 0:128]),
              [("pf", 1), ("v", 1)])
            TRB.release(tbi)

    def transpose_64wide(srcfn, dstf):
        for qt in range(2):
            mi, mb = MM.acquire()
            t(lambda EE, mb=mb, qt=qt, srcfn=srcfn: EE["tensor"].transpose(
                mb[0:128, 0:DTF], srcfn(qt), idf[0:64, 0:64]), [("t", 1)])
            b.wait("vector", "t", b.mark("t"))
            v(lambda EE, mb=mb, dstf=dstf, qt=qt: EE["vector"].tensor_copy(
                dstf[:, qt, :], mb[0:128, 0:DTF]), [("pf", 1)])
            MM.release(mi)

    convo_guard = [0, 0]

    def ssm_chunk(dt_src, dt_col0, emit_y, lc, first_chunk_S_zero):
        c0 = dt_col0 % 512
        sg_guard = [0, 0]
        g2_guard = [0, 0]
        b.wait("scalar", "t", b.mark("t"))
        chunk_t0 = b.mark("t")
        for og in range(0, OCX, 4):
            b.wait("scalar", "t", convo_guard[0])
            for j in range(4):
                conv_oc(og + j, c0, convo[:, j, :])
            b.wait("tensor", "s", b.mark("s"))
            for qt in range(2):
                tbi, tbk = TRB.acquire()
                for j in range(4):
                    t(lambda EE, tbk=tbk, j=j, qt=qt: EE["tensor"].transpose(
                        tbk[:, j * 128:(j + 1) * 128], convo[:, j, qt * 128:(qt + 1) * 128],
                        idb[:, :]), [("t", 1)] if j == 3 else ())
                b.wait("scalar", "t", b.mark("t"))
                s(lambda EE, tbk=tbk, qt=qt, og=og: EE["scalar"].copy(
                    xtm2[:, qt, og * 128:(og + 4) * 128], tbk[:, 0:512]))
                TRB.release(tbi, "scalar")
            convo_guard[0] = b.mark("t")
        conv_oc(OCB, c0, bfm[:, :])
        if emit_y:
            conv_oc(OCC, c0, cfm[:, :])
        fm_to_tm(bfm, [btm[:, 0, :], btm[:, 1, :]])

        b.wait("vector", "t", chunk_t0)
        b.wait("vector", "s", b.mark("s"))
        v(lambda EE: EE["vector"].tensor_scalar(csb[0:DTF, :], dt_src[:, dt_col0:dt_col0 + Q],
                                                acol[:, 0:1], None, ALU.mult))
        srcb, dstb = csb, csb2
        sh = 1
        while sh < Q:
            b.wait("vector", "v", b.mark("v"))
            v(lambda EE, a=dstb, c=srcb, sh=sh: EE["vector"].tensor_copy(a[0:DTF, 0:sh], c[0:DTF, 0:sh]))
            v(lambda EE, a=dstb, c=srcb, sh=sh: EE["vector"].tensor_tensor(
                a[0:DTF, sh:Q], c[0:DTF, sh:Q], c[0:DTF, 0:Q - sh], ALU.add))
            srcb, dstb = dstb, srcb
            sh *= 2
        b.wait("vector", "v", b.mark("v"))
        cs = srcb
        b.wait("tensor", "v", b.mark("v"))
        transpose_64wide(lambda qt: cs[0:DTF, qt * 128:(qt + 1) * 128], cst)
        transpose_64wide(lambda qt: dt_src[0:DTF, dt_col0 + qt * 128:dt_col0 + (qt + 1) * 128], dtt)
        v(lambda EE: EE["vector"].tensor_scalar(
            ncst.rearrange("p a h -> p (a h)"), cst.rearrange("p a h -> p (a h)"),
            -1.0, None, ALU.mult), [("v", 1)])
        if emit_y:
            b.wait("scalar", "v", b.mark("v"))
            s(lambda EE: EE["scalar"].activation(rfT[:, 0, :], cst[:, 0, :], AF.Exp))
            s(lambda EE: EE["scalar"].activation(rfT[:, 1, :], cst[:, 1, :], AF.Exp), [("s", 1)])
        mi0, mb0 = MM.acquire()
        t(lambda EE, mb0=mb0, cs=cs: EE["tensor"].transpose(
            mb0[0:1, 0:DTF], cs[0:DTF, Q - 1:Q], idf[0:DTF, 0:DTF]), [("t", 1)])
        b.wait("vector", "t", b.mark("t"))
        v(lambda EE, mb0=mb0: EE["vector"].tensor_copy(t64a[0:1, 0:DTF], mb0[0:1, 0:DTF]),
          [("pf", 1), ("v", 1)])
        MM.release(mi0)
        b.wait("tensor", "v", b.mark("v"))
        mi, mb = MM.acquire()
        t(lambda EE, mb=mb: EE["tensor"].matmul(mb[:, 0:DTF], ones64[0:1, :], t64a[0:1, 0:DTF],
                                                start=True, stop=True), [("t", 1)])
        b.wait("vector", "t", b.mark("t"))
        for qt in range(2):
            v(lambda EE, mb=mb, qt=qt: EE["vector"].tensor_tensor(
                wsT[:, qt, :], mb[:, 0:DTF], cst[:, qt, :], ALU.subtract),
              [("pf", 1), ("v", 1)] if qt == 1 else ())
        MM.release(mi)
        b.wait("scalar", "v", b.mark("v"))
        s(lambda EE: EE["scalar"].activation(wsT[:, 0, :], wsT[:, 0, :], AF.Exp))
        s(lambda EE: EE["scalar"].activation(wsT[:, 1, :], wsT[:, 1, :], AF.Exp), [("s", 1)])
        b.wait("vector", "s", b.mark("s"))
        v(lambda EE: EE["vector"].tensor_tensor(wsT[:, 0, :], wsT[:, 0, :], dtt[:, 0, :], ALU.mult))
        v(lambda EE: EE["vector"].tensor_tensor(wsT[:, 1, :], wsT[:, 1, :], dtt[:, 1, :], ALU.mult),
          [("v", 1)])
        s(lambda EE: EE["scalar"].activation(t64a[0:1, 0:DTF], t64a[0:1, 0:DTF], AF.Exp), [("s", 1)])
        b.wait("tensor", "s", b.mark("s"))
        cdi, cdb = SMH.acquire()
        t(lambda EE, cdb=cdb: EE["tensor"].matmul(cdb[:, 0:DTF], ones64[0:1, :],
                                                  t64a[0:1, 0:DTF], start=True, stop=True),
          [("t", 1)])
        cd_t = b.mark("t")

        if emit_y:
            mi, mb = MM.acquire()
            t(lambda EE, mb=mb: EE["tensor"].matmul(mb[:, 0:DTF], ones64[0:1, :], cst[0:1, 1, :],
                                                    start=True, stop=True), [("t", 1)])
            b.wait("vector", "t", b.mark("t"))
            v(lambda EE, mb=mb: EE["vector"].tensor_tensor(woT_[:, :], mb[:, 0:DTF],
                                                           cst[:, 0, :], ALU.subtract))
            v(lambda EE, mb=mb: EE["vector"].tensor_tensor(roT_[:, :], cst[:, 1, :],
                                                           mb[:, 0:DTF], ALU.subtract),
              [("pf", 1), ("v", 1)])
            MM.release(mi)
            b.wait("scalar", "v", b.mark("v"))
            s(lambda EE: EE["scalar"].activation(woT_[:, :], woT_[:, :], AF.Exp))
            s(lambda EE: EE["scalar"].activation(roT_[:, :], roT_[:, :], AF.Exp), [("s", 1)])
            b.wait("vector", "s", b.mark("s"))
            v(lambda EE: EE["vector"].tensor_tensor(woT_[:, :], woT_[:, :], dtt[:, 0, :], ALU.mult),
              [("v", 1)])
            for st in range(2):
                mi, mb = MM.acquire()
                q0 = 0 if st == 0 else 128
                t(lambda EE, mb=mb, st=st, q0=q0: EE["tensor"].matmul(
                    mb[:, 0:Q - q0], bfm[:, st * 128:(st + 1) * 128], cfm[:, q0:Q],
                    start=True, stop=True), [("t", 1)])
                b.wait("vector", "t", b.mark("t"))
                if st == 0:
                    v(lambda EE, mb=mb: EE["vector"].tensor_tensor(
                        cbt[:, 0, :], mb[:, 0:128], trilb[:, :], ALU.mult))
                    v(lambda EE, mb=mb: EE["vector"].tensor_copy(cbt[:, 1, :], mb[:, 128:256]),
                      [("pf", 1)])
                else:
                    v(lambda EE, mb=mb: EE["vector"].tensor_tensor(
                        cbt[:, 2, :], mb[:, 0:128], trilb[:, :], ALU.mult), [("pf", 1)])
                MM.release(mi)
            v(lambda EE: EE["vector"].tensor_tensor(
                xscr[:, :].rearrange("p (h d) -> p h d", h=DTF),
                xtm2[:, 0, :].rearrange("p (h d) -> p h d", h=DTF),
                hbc(woT_[:, :], DTF), ALU.mult), [("v", 1)])
            xoff_v = b.mark("v")

            md_guard = {}
            mdv_marks = {}

            def stageA(i):
                qt, fb = divmod(i, 8)
                par = i % 2
                md = md8[:, par, :, :]
                b.wait("scalar", "t", md_guard.get(par, 0))
                for hh in range(8):
                    h = fb * 8 + hh
                    smi, smb = MM.acquire()
                    t(lambda EE, smb=smb, h=h, qt=qt, cs=cs: EE["tensor"].matmul(
                        smb[:, 0:128], idf[0:DTF, h:h + 1].broadcast_to([DTF, 128]),
                        cs[0:DTF, qt * 128:(qt + 1) * 128], start=True, stop=True),
                      [("t", 1)])
                    b.wait("scalar", "t", b.mark("t"))
                    uscr = sgt.rearrange("p a b -> p (a b)")
                    s(lambda EE, smb=smb, hh=hh, h=h, qt=qt, uscr=uscr: EE["scalar"].activation(
                        uscr[:, hh * 128:(hh + 1) * 128],
                        smb[:, 0:128], AF.Relu, bias=cst[:, qt, h:h + 1], scale=-1.0))
                    MM.release(smi, "scalar")
                    s(lambda EE, hh=hh, md=md, uscr=uscr: EE["scalar"].activation(
                        md[:, hh, :], uscr[:, hh * 128:(hh + 1) * 128],
                        AF.Exp, scale=-1.0), [("s", 1)])
                b.wait("vector", "s", b.mark("s"))
                for hh in range(8):
                    h = fb * 8 + hh
                    v(lambda EE, hh=hh, h=h, qt=qt, md=md: EE["vector"].scalar_tensor_tensor(
                        md[:, hh, :], md[:, hh, :], dtt[:, qt, h:h + 1],
                        cbt[:, 2 * qt, :], ALU.mult, ALU.mult), [("v", 1)] if hh == 7 else ())
                mdv_marks[i] = b.mark("v")

            def stageB(i):
                qt, fb = divmod(i, 8)
                par = i % 2
                md = md8[:, par, :, :]
                md_v = mdv_marks[i]
                ia, ba = MM.acquire()
                t(lambda EE, ba=ba, qt=qt, fb=fb: EE["tensor"].matmul(
                    ba[:, 0:512], cfm[:, qt * 128:(qt + 1) * 128],
                    sst[:, fb * 512:(fb + 1) * 512], start=True, stop=True), [("t", 1)])
                ib, bb = MM.acquire()
                b.wait("tensor", "v", md_v)
                for hh in range(8):
                    h = fb * 8 + hh
                    t(lambda EE, bb=bb, qt=qt, hh=hh, h=h, md=md: EE["tensor"].matmul(
                        bb[:, hh * 64:(hh + 1) * 64], md[:, hh, :],
                        xtm2[:, qt, h * 64:(h + 1) * 64],
                        start=(hh == 0), stop=(hh == 7)),
                      [("t", 1)] if hh == 7 else ())
                if qt == 1:
                    ic, bc_ = MM.acquire()
                    b.wait("tensor", "v", xoff_v)
                    t(lambda EE, bc_=bc_, fb=fb: EE["tensor"].matmul(
                        bc_[:, 0:512], cbt[:, 1, :], xscr[:, fb * 512:(fb + 1) * 512],
                        start=True, stop=True), [("t", 1)])
                b.wait("vector", "t", b.mark("t"))
                ysl = ytm[:, fb * 512:(fb + 1) * 512].rearrange("p (h d) -> p h d", h=8)
                v(lambda EE, ba=ba, ysl=ysl, qt=qt, fb=fb: EE["vector"].tensor_tensor(
                    ysl, ba[:, 0:512].rearrange("p (h d) -> p h d", h=8),
                    hbc(rfT[:, qt, fb * 8:(fb + 1) * 8], 8), ALU.mult), [("pf", 1)])
                MM.release(ia)
                v(lambda EE, bb=bb, fb=fb: EE["vector"].tensor_tensor(
                    ytm[:, fb * 512:(fb + 1) * 512], ytm[:, fb * 512:(fb + 1) * 512],
                    bb[:, 0:512], ALU.add), [("pf", 1)])
                MM.release(ib)
                v(lambda EE, qt=qt, fb=fb: EE["vector"].tensor_tensor(
                    scrv[:, 0:512].rearrange("p (h d) -> p h d", h=8),
                    xtm2[:, qt, fb * 512:(fb + 1) * 512].rearrange("p (h d) -> p h d", h=8),
                    hbc(dbct[:, fb * 8:(fb + 1) * 8], 8), ALU.mult))
                v(lambda EE, fb=fb: EE["vector"].tensor_tensor(
                    ytm[:, fb * 512:(fb + 1) * 512], ytm[:, fb * 512:(fb + 1) * 512],
                    scrv[:, 0:512], ALU.add))
                if qt == 1:
                    v(lambda EE, bc_=bc_, fb=fb: EE["vector"].tensor_tensor(
                        scrv[:, 0:512].rearrange("p (h d) -> p h d", h=8),
                        bc_[:, 0:512].rearrange("p (h d) -> p h d", h=8),
                        hbc(roT_[:, fb * 8:(fb + 1) * 8], 8), ALU.mult), [("pf", 1)])
                    MM.release(ic)
                    v(lambda EE, fb=fb: EE["vector"].tensor_tensor(
                        ytm[:, fb * 512:(fb + 1) * 512], ytm[:, fb * 512:(fb + 1) * 512],
                        scrv[:, 0:512], ALU.add))
                md_guard[par] = b.mark("t")

            for qt in range(2):
                if qt == 0:
                    stageA(0)
                for fb in range(8):
                    i = qt * 8 + fb
                    if i + 1 < 16:
                        stageA(i + 1)
                    stageB(i)
                b.wait("tensor", "v", b.mark("v"))
                for grp in range(4):
                    tbi, tbk = TRB.acquire()
                    for kk in range(8):
                        oc = grp * 8 + kk
                        t(lambda EE, tbk=tbk, kk=kk, oc=oc: EE["tensor"].transpose(
                            tbk[:, kk * 128:(kk + 1) * 128], ytm[:, oc * 128:(oc + 1) * 128],
                            idb[:, :]), [("t", 1)] if kk == 7 else ())
                    b.wait("vector", "t", b.mark("t"))
                    col = qt * 128
                    v(lambda EE, tbk=tbk, grp=grp, col=col: EE["vector"].tensor_copy(
                        ypair[:, grp * 8:(grp + 1) * 8, col:col + 128],
                        tbk[:, 0:1024].rearrange("p (a b2) -> p a b2", a=8)), [("pf", 1)])
                    TRB.release(tbi)

        if emit_y:
            tok0 = dt_col0
            ss_i, ss_b = SML.acquire()
            for oc in range(32):
                o0 = oc * 128
                wi = wload(lambda o0=o0: dv["wT"][:, o0:o0 + 128].rearrange("(c i) o -> i c o", i=128))
                b.wait("tensor", wstate["marks"][wi][0], wstate["marks"][wi][1])
                ws = wslice(wi, KC, 128)
                mi, mb = MM.acquire()
                for k in range(KC):
                    t(lambda EE, mb=mb, ws=ws, k=k: EE["tensor"].matmul(
                        mb[:, 0:Q], ws[:, k, 0:128], xn[:, k, tok0:tok0 + Q],
                        start=(k == 0), stop=(k == KC - 1)), [("t", 1)] if k == KC - 1 else ())
                wrel(wi)
                pr = oc % 2
                b.wait("scalar", "t", b.mark("t"))
                b.wait("scalar", "v", sg_guard[pr])
                s(lambda EE, mb=mb, pr=pr: EE["scalar"].activation(
                    sgt[:, pr, 0:Q], mb[:, 0:Q], AF.Silu))
                MM.release(mi, "scalar")
                b.wait("vector", "s", b.mark("s"))
                b.wait("vector", "t", g2_guard[pr])
                v(lambda EE, oc=oc, pr=pr: EE["vector"].tensor_tensor(
                    ypair[:, oc, :], ypair[:, oc, :], sgt[:, pr, 0:Q], ALU.mult))
                v(lambda EE, oc=oc, pr=pr: EE["vector"].tensor_tensor(
                    g2t[:, pr, 0:Q], ypair[:, oc, :], ypair[:, oc, :], ALU.mult), [("v", 1)])
                sg_guard[pr] = b.mark("v")
                b.wait("tensor", "v", b.mark("v"))
                t(lambda EE, ss_b=ss_b, oc=oc, pr=pr: EE["tensor"].matmul(
                    ss_b[0:1, 0:Q], onescol[:, 0:1], g2t[:, pr, 0:Q],
                    start=(oc == 0), stop=(oc == 31)), [("t", 1)] if oc == 31 else ())
                g2_guard[pr] = b.mark("t")
            b.wait("vector", "t", b.mark("t"))
            v(lambda EE, ss_b=ss_b: EE["vector"].tensor_scalar(
                rr[0:1, 0:Q], ss_b[0:1, 0:Q], 1.0 / I, EPS, ALU.mult, ALU.add), [("v", 1)])
            SML.release(ss_i)
            b.wait("scalar", "v", b.mark("v"))
            s(lambda EE: EE["scalar"].activation(rr[0:1, 0:Q], rr[0:1, 0:Q], AF.Sqrt), [("s", 1)])
            b.wait("vector", "s", b.mark("s"))
            vred(lambda EE: EE["vector"].reciprocal(rr[0:1, 0:Q], rr[0:1, 0:Q]))
            b.wait("tensor", "v", b.mark("v"))
            rs_i, rs_b = SML.acquire()
            t(lambda EE, rs_b=rs_b: EE["tensor"].matmul(rs_b[:, 0:Q], ones64[0:1, :],
                                                        rr[0:1, 0:Q], start=True, stop=True),
              [("t", 1)])
            b.wait("vector", "t", b.mark("t"))
            for oc in range(OCX):
                v(lambda EE, oc=oc, rs_b=rs_b: EE["vector"].scalar_tensor_tensor(
                    ypair[:, oc, :], ypair[:, oc, :], nwpc[:, oc:oc + 1], rs_b[:, 0:Q],
                    ALU.mult, ALU.mult), [("v", 1)] if oc == OCX - 1 else ())
            SML.release(rs_i)
            gn_v = b.mark("v")
            b.wait("tensor", "v", gn_v)
            for oc in range(KC):
                o0 = oc * 128
                wis = [wload(lambda o0=o0, hk=hk: dv["woT"][hk * 2048:(hk + 1) * 2048, o0:o0 + 128]
                             .rearrange("(c i) o -> i c o", i=128)) for hk in range(2)]
                mi, mb = MM.acquire()
                for hk in range(2):
                    b.wait("tensor", wstate["marks"][wis[hk]][0], wstate["marks"][wis[hk]][1])
                    ws = wslice(wis[hk], 16, 128)
                    for kk in range(16):
                        k = hk * 16 + kk
                        t(lambda EE, mb=mb, ws=ws, kk=kk, k=k: EE["tensor"].matmul(
                            mb[:, 0:Q], ws[:, kk, 0:128], ypair[:, k, :],
                            start=(k == 0), stop=(k == 31)), [("t", 1)] if k == 31 else ())
                    wrel(wis[hk])
                b.wait("vector", "t", b.mark("t"))
                v(lambda EE, mb=mb, oc=oc, tok0=tok0: EE["vector"].tensor_copy(
                    outacc[:, oc, tok0:tok0 + Q], mb[:, 0:Q]), [("pf", 1)])
                MM.release(mi)

        for pp in range(4):
            b.wait("vector", "t", b.mark("t"))
            for half in range(2):
                for sub in range(2):
                    fb = pp * 2 + half
                    v(lambda EE, sub=sub, fb=fb, half=half: EE["vector"].tensor_tensor(
                        xscr[:, (half * 2 + sub) * 1024:(half * 2 + sub) * 1024 + 512]
                        .rearrange("p (h d) -> p h d", h=8),
                        xtm2[:, sub, fb * 512:(fb + 1) * 512].rearrange("p (h d) -> p h d", h=8),
                        hbc(wsT[:, sub, fb * 8:(fb + 1) * 8], 8), ALU.mult),
                      [("v", 1)] if (half, sub) == (1, 1) else ())
            stv = b.mark("v")
            bks = []
            for half in range(2):
                mi, mb = MM.acquire()
                bks.append((mi, mb))
                if half == 0:
                    b.wait("tensor", "v", stv)
                for sub in range(2):
                    t(lambda EE, mb=mb, half=half, sub=sub: EE["tensor"].matmul(
                        mb[:, 0:512], btm[:, sub, :],
                        xscr[:, (half * 2 + sub) * 1024:(half * 2 + sub) * 1024 + 512],
                        start=(sub == 0), stop=(sub == 1)),
                      [("t", 1)] if sub == 1 else ())
            b.wait("vector", "t", b.mark("t"))
            if pp == 0:
                b.wait("vector", "t", cd_t)
            for half in range(2):
                fb = pp * 2 + half
                mi, mb = bks[half]
                ssl = sst[:, fb * 512:(fb + 1) * 512].rearrange("p (h d) -> p h d", h=8)
                if first_chunk_S_zero:
                    v(lambda EE, mb=mb, fb=fb: EE["vector"].tensor_copy(
                        sst[:, fb * 512:(fb + 1) * 512], mb[:, 0:512]), [("pf", 1)])
                else:
                    v(lambda EE, ssl=ssl, fb=fb, cdb=cdb: EE["vector"].tensor_tensor(
                        ssl, ssl, hbc(cdb[:, fb * 8:(fb + 1) * 8], 8), ALU.mult))
                    v(lambda EE, mb=mb, fb=fb: EE["vector"].tensor_tensor(
                        sst[:, fb * 512:(fb + 1) * 512], sst[:, fb * 512:(fb + 1) * 512],
                        mb[:, 0:512], ALU.add), [("pf", 1)])
                MM.release(mi)
        SMH.release(cdi)

    inproj_xbc(xnh, 0, HALO, dthB, 0)
    for ti in range(8):
        ln_tile(ti, dv["xm"][ti * 128:(ti + 1) * 128, :], xn, ti * 128)
    v(lambda EE: EE["vector"].memset(biga[:, :, 0:3], 0.0), [("v", 1)])
    ssm_chunk(dthB, 0, False, 0, True)
    v(lambda EE: EE["vector"].tensor_scalar(sst[:, :], sst[:, :], hscs[:, 0:1], None, ALU.mult))
    v(lambda EE: EE["vector"].tensor_scalar(
        lead3[:, :, :], biga[:, :, HALO:HALO + 3], hscs[:, 0:1], None, ALU.mult),
      [("v", 1)])

    inproj_xbc(xn, 0, 512, dtmB, 0)
    v(lambda EE: EE["vector"].tensor_copy(biga[:, :, 0:3], lead3[:, :, :]), [("v", 1)])
    ssm_chunk(dtmB, 0, True, 0, False)
    ssm_chunk(dtmB, 256, True, 1, False)
    v(lambda EE: EE["vector"].tensor_copy(lead3[:, :, :], biga[:, :, 512:515]),
      [("v", 1)])

    inproj_xbc(xn, 512, 512, dtmB, 512)
    v(lambda EE: EE["vector"].tensor_copy(biga[:, :, 0:3], lead3[:, :, :]), [("v", 1)])
    ssm_chunk(dtmB, 512, True, 0, False)
    ssm_chunk(dtmB, 768, True, 1, False)

    for tt in range(8):
        smi, smb = SML.acquire()
        for k in range(KC):
            t(lambda EE, smb=smb, k=k, tt=tt: EE["tensor"].matmul(
                smb[:, 0:E], xn[:, k, tt * 128:(tt + 1) * 128], rw8[:, k, :],
                start=(k == 0), stop=(k == KC - 1)), [("t", 1)] if k == KC - 1 else ())
        b.wait("vector", "t", b.mark("t"))
        v(lambda EE, smb=smb: EE["vector"].tensor_copy(lgt[:, 0:8], smb[:, 0:E]), [("pf", 1)])
        SML.release(smi)
        vred(lambda EE: EE["vector"].reduce_max(m12[:, 1:2], lgt[:, 0:8], axis=AXX, negate=True))
        b.wait("scalar", "v", b.mark("v"))
        s(lambda EE: EE["scalar"].activation(lgt[:, 8:16], lgt[:, 0:8], AF.Exp,
                                             bias=m12[:, 1:2]), [("s", 1)])
        b.wait("vector", "s", b.mark("s"))
        vred(lambda EE: EE["vector"].reduce_sum(m12[:, 2:3], lgt[:, 8:16], axis=AXX))
        vred(lambda EE: EE["vector"].reciprocal(m12[:, 2:3], m12[:, 2:3]))
        v(lambda EE: EE["vector"].tensor_scalar(lgt[:, 8:16], lgt[:, 8:16], m12[:, 2:3],
                                                None, ALU.mult))
        vred(lambda EE: EE["vector"].reduce_max(m12[:, 0:1], lgt[:, 8:16], axis=AXX))
        v(lambda EE: EE["vector"].tensor_scalar(lgt[:, 0:8], lgt[:, 8:16], m12[:, 0:1],
                                                None, ALU.is_equal))
        b.wait("vector", "v", b.mark("v"))
        v(lambda EE: EE["vector"].scalar_tensor_tensor(scrv[:, 0:8], lgt[:, 0:8], -1e30,
                                                       lgt[:, 8:16], ALU.mult, ALU.add))
        vred(lambda EE: EE["vector"].reduce_max(m12[:, 1:2], scrv[:, 0:8], axis=AXX))
        v(lambda EE: EE["vector"].tensor_scalar(scrv[:, 8:16], scrv[:, 0:8], m12[:, 1:2],
                                                None, ALU.is_equal))
        b.wait("vector", "v", b.mark("v"))
        v(lambda EE: EE["vector"].tensor_tensor(lgt[:, 0:8], lgt[:, 0:8], scrv[:, 8:16], ALU.add))
        b.wait("vector", "v", b.mark("v"))
        v(lambda EE: EE["vector"].tensor_tensor(lgt[:, 0:8], lgt[:, 0:8], lgt[:, 8:16], ALU.mult))
        v(lambda EE: EE["vector"].tensor_tensor(m12[:, 3:4], m12[:, 0:1], m12[:, 1:2], ALU.add))
        vred(lambda EE: EE["vector"].reciprocal(m12[:, 3:4], m12[:, 3:4]))
        b.wait("vector", "v", b.mark("v"))
        v(lambda EE, tt=tt: EE["vector"].tensor_scalar(comb[:, tt * 8:(tt + 1) * 8], lgt[:, 0:8],
                                                       m12[:, 3:4], None, ALU.mult), [("v", 1)])
        b.wait("tensor", "v", b.mark("v"))
        mi, mb = MM.acquire()
        t(lambda EE, mb=mb, tt=tt: EE["tensor"].transpose(
            mb[0:E, 0:128], comb[:, tt * 8:(tt + 1) * 8], idf[:, :]), [("t", 1)])
        b.wait("vector", "t", b.mark("t"))
        v(lambda EE, mb=mb, tt=tt: EE["vector"].tensor_copy(
            combT[0:E, tt * 128:(tt + 1) * 128], mb[0:E, 0:128]), [("pf", 1)])
        MM.release(mi)

    amoe_read_t = 0
    for ee in range(E):
        cb_is = []
        for th in range(2):
            smi, smb = (SML if th == 0 else SMH).acquire()
            t(lambda EE, smb=smb, ee=ee, th=th: EE["tensor"].matmul(
                smb[:, 0:512], idb[0:E, ee:ee + 1].broadcast_to([E, 128]),
                combT[0:E, th * 512:(th + 1) * 512], start=True, stop=True), [("t", 1)])
            cb_is.append((smi, smb))
        cb_t = b.mark("t")
        b.wait("vector", "t", cb_t)
        sgj_guard = [0, 0]
        for j in range(JC):
            par = j % 2
            for role in range(2):
                oc = j if role == 0 else JC + j
                o0 = oc * 128
                wi = wload(lambda o0=o0, ee=ee: dv["gupT"][ee, :, o0:o0 + 128]
                           .rearrange("(c i) o -> i c o", i=128))
                b.wait("tensor", wstate["marks"][wi][0], wstate["marks"][wi][1])
                ws = wslice(wi, KC, 128)
                for th in range(2):
                    mi, mb = MM.acquire()
                    for k in range(KC):
                        incs = [("t", 1)] if k == KC - 1 else []
                        t(lambda EE, mb=mb, ws=ws, k=k, th=th: EE["tensor"].matmul(
                            mb[:, 0:512], ws[:, k, 0:128],
                            xn[:, k, th * 512:(th + 1) * 512],
                            start=(k == 0), stop=(k == KC - 1)), incs)
                    if role == 0:
                        b.wait("scalar", "t", b.mark("t"))
                        if th == 0:
                            b.wait("scalar", "v", sgj_guard[par])
                        s(lambda EE, mb=mb, par=par, th=th: EE["scalar"].activation(
                            sgj[:, par, th, :], mb[:, 0:512], AF.Silu))
                        MM.release(mi, "scalar")
                        b.wait("vector", "s", b.mark("s"))
                        v(lambda EE, par=par, th=th, cb=cb_is[th][1]: EE["vector"].tensor_tensor(
                            sgj[:, par, th, :], sgj[:, par, th, :], cb[:, 0:512],
                            ALU.mult), [("v", 1)])
                    else:
                        b.wait("vector", "t", b.mark("t"))
                        b.wait("vector", "t", amoe_read_t)
                        v(lambda EE, mb=mb, j2=j, th=th, par=par: EE["vector"].tensor_tensor(
                            amoe[:, j2, th * 512:(th + 1) * 512], mb[:, 0:512],
                            sgj[:, par, th, :], ALU.mult), [("pf", 1), ("v", 1)])
                        MM.release(mi)
                if role == 1:
                    sgj_guard[par] = b.mark("v")
                wrel(wi)
        SML.release(cb_is[0][0])
        SMH.release(cb_is[1][0])
        a_v = b.mark("v")
        b.wait("tensor", "v", a_v)
        for oc in range(KC):
            o0 = oc * 128
            wi = wload(lambda o0=o0, ee=ee: dv["downT"][ee, :, o0:o0 + 128]
                       .rearrange("(c i) o -> i c o", i=128))
            b.wait("tensor", wstate["marks"][wi][0], wstate["marks"][wi][1])
            if True:
                ws = wslice(wi, JC, 128)
                for th in range(2):
                    mi, mb = MM.acquire()
                    for k in range(JC):
                        incs = [("t", 1)] if k == JC - 1 else []
                        t(lambda EE, mb=mb, ws=ws, k=k, th=th: EE["tensor"].matmul(
                            mb[:, 0:512], ws[:, k, 0:128],
                            amoe[:, k, th * 512:(th + 1) * 512],
                            start=(k == 0), stop=(k == JC - 1)), incs)
                    b.wait("vector", "t", b.mark("t"))
                    v(lambda EE, mb=mb, oc=oc, th=th: EE["vector"].tensor_tensor(
                        outacc[:, oc, th * 512:(th + 1) * 512],
                        outacc[:, oc, th * 512:(th + 1) * 512], mb[:, 0:512], ALU.add),
                      [("pf", 1)])
                    MM.release(mi)
                wrel(wi)
        amoe_read_t = b.mark("t")

    b.wait("tensor", "v", b.mark("v"))
    moe_v = b.mark("v")
    x5_free = [moe_v, moe_v]
    for tt in range(8):
        hb = tt % 2
        b.wait("sync", "v", x5_free[hb])
        xs_ = f"x{hb}"
        sy(lambda EE, hb=hb, tt=tt: EE["sync"].dma_start(
            out=xld5[hb][:, 0:2048], in_=dv["xm"][tt * 128:(tt + 1) * 128, :]), [(xs_, 16)])
        xmark = b.mark(xs_)
        for grp in range(2):
            tbi, tbk = TRB.acquire()
            for kk in range(8):
                oc = grp * 8 + kk
                t(lambda EE, tbk=tbk, kk=kk, oc=oc, tt=tt: EE["tensor"].transpose(
                    tbk[:, kk * 128:(kk + 1) * 128], outacc[:, oc, tt * 128:(tt + 1) * 128],
                    idb[:, :]), [("t", 1)] if kk == 7 else ())
            b.wait("vector", "t", b.mark("t"))
            if grp == 0:
                b.wait("vector", xs_, xmark)
            v(lambda EE, tbk=tbk, grp=grp, hb=hb: EE["vector"].tensor_tensor(
                xout5[hb][:, grp * 1024:(grp + 1) * 1024], xld5[hb][:, grp * 1024:(grp + 1) * 1024],
                tbk[:, 0:1024], ALU.add), [("pf", 1), ("v", 1)])
            TRB.release(tbi)
        x5_free[hb] = b.mark("v")
        b.wait("sync", "v", b.mark("v"))
        sy(lambda EE, hb=hb, tt=tt: EE["sync"].dma_start(
            out=dv["out"][tt * 128:(tt + 1) * 128, :], in_=xout5[hb][:, 0:2048]), [("o", 16)])
    b.wait("sync", "o", b.mark("o"))

    waited = {}
    for eng in ENGINES:
        for fn, incs in b.ops[eng]:
            if isinstance(fn, tuple) and fn[0] == "wait":
                waited.setdefault(fn[1], set()).add(fn[2])
    cnt2 = {}
    keep = {}
    newval = {}
    cum = {}
    for eng in ENGINES:
        for fn, incs in b.ops[eng]:
            if isinstance(fn, tuple) and fn[0] == "wait":
                continue
            for sname, n in incs:
                pre = cnt2.get(sname, 0)
                post = pre + n
                cnt2[sname] = post
                ws = [w for w in waited.get(sname, ()) if pre < w <= post]
                if ws or n == 16:
                    keep[(sname, post)] = True
                    cum[sname] = cum.get(sname, 0) + n
                    for w in ws:
                        newval[(sname, w)] = cum[sname]
    cnt3 = {}
    newops = {}
    for eng in ENGINES:
        lst = []
        for fn, incs in b.ops[eng]:
            if isinstance(fn, tuple) and fn[0] == "wait":
                lst.append(((fn[0], fn[1], newval[(fn[1], fn[2])]), ()))
                continue
            ni = []
            for sname, n in incs:
                pre = cnt3.get(sname, 0)
                post = pre + n
                cnt3[sname] = post
                if keep.get((sname, post)):
                    ni.append((sname, n))
            lst.append((fn, tuple(ni)))
        newops[eng] = lst
    b.ops = newops

    with nc.Block() as block:
        def replay(eng_name):
            def f(EO):
                EE = {eng_name: EO}
                for fn, incs in b.ops[eng_name]:
                    if isinstance(fn, tuple) and fn[0] == "wait":
                        EO.wait_ge(sems[fn[1]], fn[2])
                        continue
                    inst = fn(EE)
                    for sname, n in incs:
                        inst.then_inc(sems[sname], n)
            return f

        block.tensor(replay("tensor"))
        block.vector(replay("vector"))
        block.scalar(replay("scalar"))
        block.gpsimd(replay("gpsimd"))
        block.sync(replay("sync"))

    es.close()
    return nc


_NC_CACHE = {}


def kernel(**inputs):
    x = np.asarray(inputs["L0_attn_hidden_states"], dtype=np.float32)
    wT = np.ascontiguousarray(np.asarray(inputs["L0_attn_in_proj_w"], np.float32).T)
    woT = np.ascontiguousarray(np.asarray(inputs["L0_attn_out_proj_w"], np.float32).T)
    gupT = np.ascontiguousarray(np.asarray(inputs["L0_ffn_gate_up_proj"], np.float32).transpose(0, 2, 1))
    downT = np.ascontiguousarray(np.asarray(inputs["L0_ffn_down_proj"], np.float32).transpose(0, 2, 1))
    routT = np.ascontiguousarray(np.asarray(inputs["L0_ffn_router_w"], np.float32).T)
    conv_w = np.asarray(inputs["L0_attn_conv_w"], np.float32)
    conv_b = np.asarray(inputs["L0_attn_conv_b"], np.float32)
    dt_bias = np.asarray(inputs["L0_attn_dt_bias"], np.float32)
    A = np.asarray(inputs["L0_attn_A"], np.float32)
    D = np.asarray(inputs["L0_attn_D"], np.float32)
    nw = np.asarray(inputs["L0_attn_norm_w"], np.float32)

    convw_pc = np.ascontiguousarray(conv_w.reshape(NOC, 128, 4).transpose(1, 0, 2).reshape(128, NOC * 4))
    convb_pc = np.ascontiguousarray(conv_b.reshape(NOC, 128).T)
    nw_pc = np.ascontiguousarray(nw.reshape(OCX, 128).T)

    if "nc" not in _NC_CACHE:
        _NC_CACHE["nc"] = build_nc()
    nc = _NC_CACHE["nc"]

    in_maps = []
    for c in range(8):
        sq, qq = c // 4, c % 4
        xm = np.ascontiguousarray(x[sq, qq * TM:(qq + 1) * TM])
        if qq == 0:
            xh = np.ascontiguousarray(x[sq, 0:HALO])
            hs = 0.0
        else:
            xh = np.ascontiguousarray(x[sq, qq * TM - HALO:qq * TM])
            hs = 1.0
        in_maps.append({
            "xm": xm, "xh": xh, "hsc": np.full((128, 1), hs, np.float32),
            "wT": wT, "woT": woT, "gupT": gupT, "downT": downT, "routT": routT,
            "convw": convw_pc, "convb": convb_pc, "nwpc": nw_pc,
            "acol": A.reshape(DTF, 1), "dtb": dt_bias.reshape(DTF, 1),
            "drow": D.reshape(1, DTF),
        })
    res = run_bass_kernel_spmd(nc, in_maps, core_ids=list(range(8)))
    out = np.zeros((2, 4096, HID), np.float32)
    for c in range(8):
        sq, qq = c // 4, c % 4
        out[sq, qq * TM:(qq + 1) * TM] = res.results[c]["out"]
    return out


# version 29 (final)
# speedup vs baseline: 1.0511x; 1.0511x over previous
import numpy as np
import ml_dtypes
from contextlib import ExitStack
import concourse.bass as bass
import concourse.mybir as mybir
from concourse.bass_utils import run_bass_kernel_spmd

F32 = mybir.dt.float32
BF16 = mybir.dt.bfloat16
AF = mybir.ActivationFunctionType
ALU = mybir.AluOpType
AXX = mybir.AxisListType.X

HID = 2048
KC = HID // 128
I = 4096
NF = 128
CONV = I + 2 * NF
DTF = 64
PD = 64
Q = 256
TM = 1024
HALO = 256
EPS = 1e-5
OCX = 32
OCB = 32
OCC = 33
NOC = 34
JC = 11
E = 8
ENGINES = ("tensor", "vector", "scalar", "gpsimd", "sync")


AUTOSEM = {"tensor": "t", "vector": "v", "scalar": "s", "gpsimd": "g", "sync": None}


class B:

    def __init__(self):
        self.ops = {e: [] for e in ENGINES}
        self.cnt = {}

    def op(self, eng, fn, incs=()):
        if incs and incs[0][1] == 16:
            use = tuple(incs[:1])
        else:
            a = AUTOSEM[eng]
            use = ((a, 1),) if a else ()
        for sname, n in use:
            self.cnt[sname] = self.cnt.get(sname, 0) + n
        self.ops[eng].append((fn, use))

    def wait(self, eng, sem, val=None):
        vv = self.cnt.get(sem, 0) if val is None else val
        if vv <= 0:
            return
        self.ops[eng].append((("wait", sem, vv), ()))

    def mark(self, sem):
        return self.cnt.get(sem, 0)

    def retro(self, eng, incs):
        pass


class Banks:

    def __init__(self, b, aps):
        self.b = b
        self.aps = aps
        self.rel = [None] * len(aps)
        self.i = 0

    def acquire(self):
        idx = self.i % len(self.aps)
        self.i += 1
        if self.rel[idx] is not None:
            sem, mk = self.rel[idx]
            self.b.wait("tensor", sem, mk)
        return idx, self.aps[idx]

    def release(self, idx, eng="vector"):
        self.rel[idx] = (AUTOSEM[eng], self.b.mark(AUTOSEM[eng]))


def build_nc():
    nc = bass.Bass("TRN2", target_bir_lowering=False, debug=False,
                   detect_race_conditions=False)
    dv = {}
    for n, shp, d in [
        ("xm", [TM, HID], F32), ("xh", [HALO, HID], F32), ("hsc", [128, 1], F32),
        ("wT", [HID, 8512], F32), ("woT", [I, HID], F32),
        ("gupT", [E, HID, 2816], F32), ("downT", [E, 1408, HID], F32),
        ("routT", [HID, E], F32),
        ("convw", [128, NOC * 4], F32), ("convb", [128, NOC], F32),
        ("nwpc", [128, OCX], F32), ("acol", [DTF, 1], F32),
        ("dtb", [DTF, 1], F32), ("drow", [1, DTF], F32),
    ]:
        dv[n] = nc.dram_tensor(n, shp, d, kind="ExternalInput").ap()
    dv["out"] = nc.dram_tensor("out", [TM, HID], F32, kind="ExternalOutput").ap()

    es = ExitStack()
    sbt = lambda n, shp, d: es.enter_context(nc.sbuf_tensor(n, shp, d)).ap()
    pst = lambda n, shp, d: es.enter_context(nc.psum_tensor(n, shp, d)).ap()

    xn = sbt("xn", [128, KC, TM], BF16)
    bigaflat = sbt("biga", [128, NOC * 515], BF16)
    biga = bigaflat.rearrange("p (a b) -> p a b", a=NOC)
    ypflat = sbt("ypair", [128, OCX * 256], BF16)
    ypair = ypflat.rearrange("p (a b) -> p a b", a=OCX)
    sgj = sbt("sgj", [128, 2, 2, 512], BF16)
    oaflat = sbt("outacc", [128, KC * TM], BF16)
    outacc = oaflat.rearrange("p (a b) -> p a b", a=KC)
    wbuf = sbt("wbuf", [128, 4, 2048], BF16)
    sst = sbt("sst", [128, I], BF16)
    xtm2 = sbt("xtm2", [128, 2, I], BF16)
    xscr = sbt("xscr", [128, I], BF16)
    ytm = sbt("ytm", [128, I], BF16)
    lead3 = sbt("lead3", [128, NOC, 3], BF16)
    convo = sbt("convo", [128, 4, Q], BF16)
    bfm = sbt("bfm", [128, Q], BF16)
    cfm = sbt("cfm", [128, Q], BF16)
    btm = sbt("btm", [128, 2, 128], BF16)
    cbt = sbt("cbt", [128, 3, 128], BF16)
    md8 = sbt("md8", [128, 2, 8, 128], BF16)
    dtmB = sbt("dtm", [DTF, TM], F32)
    dthB = None
    csb = sbt("csb", [DTF, Q], F32)
    cst = sbt("cst", [128, 2, DTF], F32)
    ncst = sbt("ncst", [128, 2, DTF], F32)
    dtt = sbt("dtt", [128, 2, DTF], F32)
    rfT = sbt("rfT", [128, 2, DTF], F32)
    wsT = sbt("wsT", [128, 2, DTF], F32)
    woT_ = sbt("woT_", [128, DTF], F32)
    roT_ = sbt("roT_", [128, DTF], F32)
    t64a = sbt("t64a", [1, DTF], F32)
    dbct = sbt("dbct", [128, DTF], F32)
    cacc = sbt("cacc", [128, 2, 256], F32)
    convw = sbt("convw_s", [128, NOC * 4], F32)
    convb = sbt("convb_s", [128, NOC], F32)
    nwpc = sbt("nwpc_s", [128, OCX], F32)
    acol = sbt("acol_s", [DTF, 1], F32)
    dtbs = sbt("dtb_s", [DTF, 1], F32)
    drow = sbt("drow_s", [1, DTF], F32)
    hscs = sbt("hsc_s", [128, 1], F32)
    rw8 = sbt("rw8", [128, KC, E], BF16)
    idb = sbt("idb", [128, 128], BF16)
    idf = sbt("idf", [128, 128], F32)
    trilb = sbt("trilb", [128, 128], BF16)
    ones64 = sbt("ones64", [64, 128], F32)
    onescol = sbt("onescol", [128, 1], BF16)
    comb = sbt("comb", [128, E * 8], F32)
    lgt = sbt("lgt", [128, 16], F32)
    m12 = sbt("m12", [128, 4], F32)
    lncol = sbt("lncol", [128, 4], F32)
    scrv = sbt("scrv", [128, 512], F32)
    rr = scrv[0:1, 0:512]
    g2t = scrv.bitcast(BF16).rearrange("p (a b) -> p a b", a=2)
    sgt = cacc.rearrange("p a b -> p (a b)").bitcast(BF16).rearrange("p (a b) -> p a b", a=2)
    combT = ytm[0:E, 0:TM]
    csb2 = scrv[0:DTF, 0:Q]

    dthB = dtmB[:, 0:HALO]
    lnscr = oaflat[:, 8192:12288].bitcast(F32)
    xout5 = [bigaflat[:, 8192 + i * 4096: 8192 + (i + 1) * 4096].bitcast(F32)
             for i in range(2)]
    xld0 = [oaflat[:, i * 4096:(i + 1) * 4096].bitcast(F32) for i in range(2)]
    xld5 = [ypflat[:, i * 4096:(i + 1) * 4096].bitcast(F32) for i in range(2)]
    xnh = ypflat[:, 0:KC * HALO].rearrange("p (k t) -> p k t", k=KC)
    amoe = bigaflat[:, 0:JC * 1024].rearrange("p (a b) -> p a b", a=JC)

    pmm = [pst(f"pmm{i}", [128, 512], F32) for i in range(4)]
    ptrb = [pst(f"ptrb{i}", [128, 1024], BF16) for i in range(2)]
    psm = [pst(f"psm{i}", [128, 512], F32) for i in range(2)]

    sems = {}
    for sname in ("x0", "x1", "w", "w0", "w1", "w2", "w3", "o", "t", "v", "s", "g", "wf", "pf"):
        sems[sname] = es.enter_context(nc.semaphore(sname))

    b = B()
    MM = Banks(b, pmm)
    TRB = Banks(b, ptrb)
    SML = Banks(b, [psm[0]])
    SMH = Banks(b, [psm[1]])

    def t(fn, incs=()):
        b.op("tensor", fn, incs)

    def v(fn, incs=()):
        b.op("vector", fn, incs)

    def s(fn, incs=()):
        b.op("scalar", fn, incs)

    def g(fn, incs=()):
        b.op("gpsimd", fn, incs)

    def sy(fn, incs=()):
        b.op("sync", fn, incs)


    def vred(fn):
        b.wait("vector", "v", b.mark("v"))
        v(fn)
        b.wait("vector", "v", b.mark("v"))

    def hbc(src2d, hs):
        return src2d.unsqueeze(-1).broadcast_to([128, hs, PD])

    wstate = {"iss": 0, "marks": {}, "relq": {}}

    NWS = 4

    def wload(src_fn):
        i = wstate["iss"]
        if i >= NWS:
            rel = wstate["relq"].get(i - NWS)
            if rel is not None:
                b.wait("gpsimd", rel[0], rel[1])
        ap = src_fn()
        kcs, oc = ap.shape[1], ap.shape[2]
        dst = wbuf[:, i % NWS, 0:kcs * oc].rearrange("p (k o) -> p k o", k=kcs)
        wsem = f"w{i % NWS}"
        b.op("gpsimd", lambda EE, d=dst, a=ap: EE["gpsimd"].dma_start(out=d, in_=a),
             [(wsem, 16)])
        wstate["marks"][i] = (wsem, b.mark(wsem))
        wstate["iss"] = i + 1
        return i

    def wrel(i):
        wstate["relq"][i] = ("t", b.mark("t"))

    def wslice(i, kcs, ocols):
        return wbuf[:, i % NWS, 0:kcs * ocols].rearrange("p (k o) -> p k o", k=kcs)

    for name, dst in (("convw", convw), ("convb", convb), ("nwpc", nwpc),
                      ("acol", acol), ("dtb", dtbs), ("drow", drow), ("hsc", hscs)):
        b.op("gpsimd", lambda EE, d=dst, n=name: EE["gpsimd"].dma_start(out=d[:, :], in_=dv[n][:, :]),
             [("w", 16)])
    g(lambda EE: EE["gpsimd"].dma_start(
        out=rw8[:, :, :], in_=dv["routT"].rearrange("(c i) o -> i c o", i=128)),
      [("w", 16)])
    g(lambda EE: EE["gpsimd"].memset(idb[:, :], 0.0))
    g(lambda EE: EE["gpsimd"].memset(idf[:, :], 0.0))
    g(lambda EE: EE["gpsimd"].memset(trilb[:, :], 0.0))
    g(lambda EE: EE["gpsimd"].memset(ones64[:, :], 1.0))
    g(lambda EE: EE["gpsimd"].memset(onescol[:, :], 1.0), [("g", 1)])
    b.wait("gpsimd", "g", b.mark("g"))
    g(lambda EE: EE["gpsimd"].affine_select(out=idb[:, :], in_=idb[:, :], compare_op=ALU.not_equal,
                                            fill=1.0, base=0, pattern=[[-1, 128]], channel_multiplier=1))
    g(lambda EE: EE["gpsimd"].affine_select(out=idf[:, :], in_=idf[:, :], compare_op=ALU.not_equal,
                                            fill=1.0, base=0, pattern=[[-1, 128]], channel_multiplier=1))
    g(lambda EE: EE["gpsimd"].affine_select(out=trilb[:, :], in_=trilb[:, :], compare_op=ALU.is_gt,
                                            fill=1.0, base=0, pattern=[[-1, 128]], channel_multiplier=1),
      [("g", 1)])
    b.wait("tensor", "g", b.mark("g"))
    b.wait("tensor", "w", b.mark("w"))
    b.wait("vector", "g", b.mark("g"))
    b.wait("vector", "w", b.mark("w"))
    b.wait("scalar", "w", b.mark("w"))

    bi, bk = SML.acquire()
    t(lambda EE, bk=bk: EE["tensor"].matmul(bk[:, 0:DTF], ones64[0:1, :], drow[0:1, :],
                                            start=True, stop=True), [("t", 1)])
    b.wait("vector", "t", b.mark("t"))
    v(lambda EE, bk=bk: EE["vector"].tensor_copy(dbct[:, :], bk[:, 0:DTF]), [("pf", 1)])
    SML.release(bi)

    half_free_mark = [0, 0]
    stg_guard = [0]

    def ln_tile(ti, src_ap, xn_dst, tcol):
        hb = ti % 2
        b.wait("sync", "v", half_free_mark[hb])
        half = xld0[hb]
        xs_ = f"x{hb}"
        sy(lambda EE, h=half, a=src_ap, : EE["sync"].dma_start(out=h[:, 0:2048], in_=a),
           [(xs_, 16)])
        b.wait("vector", xs_, b.mark(xs_))
        h = half[:, 0:2048]
        vred(lambda EE, h=h: EE["vector"].reduce_sum(lncol[:, 0:1], h, axis=AXX))
        v(lambda EE: EE["vector"].tensor_scalar(lncol[:, 1:2], lncol[:, 0:1], -1.0 / HID,
                                                None, ALU.mult))
        b.wait("vector", "v", b.mark("v"))
        v(lambda EE, h=h: EE["vector"].tensor_scalar(h, h, lncol[:, 1:2], None, ALU.add))
        v(lambda EE, h=h: EE["vector"].tensor_mul(lnscr[:, 0:2048], h, h))
        vred(lambda EE: EE["vector"].reduce_sum(lncol[:, 2:3], lnscr[:, 0:2048], axis=AXX))
        v(lambda EE: EE["vector"].tensor_scalar(lncol[:, 2:3], lncol[:, 2:3], 1.0 / HID, EPS,
                                                ALU.mult, ALU.add), [("v", 1)])
        b.wait("scalar", "v", b.mark("v"))
        s(lambda EE: EE["scalar"].activation(lncol[:, 3:4], lncol[:, 2:3], AF.Sqrt), [("s", 1)])
        b.wait("vector", "s", b.mark("s"))
        vred(lambda EE: EE["vector"].reciprocal(lncol[:, 3:4], lncol[:, 3:4]))
        stg = ytm[:, 0:2048]
        b.wait("vector", "t", stg_guard[0])
        v(lambda EE, h=h, stg=stg: EE["vector"].tensor_scalar(stg, h, lncol[:, 3:4],
                                                              None, ALU.mult), [("v", 1)])
        half_free_mark[hb] = b.mark("v")
        b.wait("tensor", "v", b.mark("v"))
        for grp in range(2):
            tbi, tbk = TRB.acquire()
            for kk in range(8):
                k = grp * 8 + kk
                t(lambda EE, tbk=tbk, kk=kk, k=k, stg=stg: EE["tensor"].transpose(
                    tbk[:, kk * 128:(kk + 1) * 128], stg[:, k * 128:(k + 1) * 128], idb[:, :]),
                  [("t", 1)] if kk == 7 else ())
            b.wait("vector", "t", b.mark("t"))
            v(lambda EE, tbk=tbk, grp=grp, xn_dst=xn_dst, tcol=tcol: EE["vector"].tensor_copy(
                xn_dst[:, grp * 8:(grp + 1) * 8, tcol:tcol + 128],
                tbk[:, 0:1024].rearrange("p (k t2) -> p k t2", k=8)), [("pf", 1), ("v", 1)])
            TRB.release(tbi)
        stg_guard[0] = b.mark("t")

    for ti in range(2):
        ln_tile(8 + ti, dv["xh"][ti * 128:(ti + 1) * 128, :], xnh, ti * 128)

    def inproj_xbc(xsrc, tok0, twid, dt_dst, dt_col0):
        for oc in range(NOC):
            o0 = I + oc * 128
            wi = wload(lambda o0=o0: dv["wT"][:, o0:o0 + 128].rearrange("(c i) o -> i c o", i=128))
            b.wait("tensor", wstate["marks"][wi][0], wstate["marks"][wi][1])
            ws = wslice(wi, KC, 128)
            mi, mb = MM.acquire()
            for k in range(KC):
                incs = [("t", 1)] if k == KC - 1 else []
                t(lambda EE, mb=mb, ws=ws, k=k: EE["tensor"].matmul(
                    mb[:, 0:twid], ws[:, k, 0:128],
                    xsrc[:, k, tok0:tok0 + twid], start=(k == 0), stop=(k == KC - 1)),
                  incs)
            b.wait("vector", "t", b.mark("t"))
            v(lambda EE, mb=mb, oc=oc, twid=twid: EE["vector"].tensor_copy(
                biga[:, oc, 3:3 + twid], mb[:, 0:twid]), [("pf", 1)])
            MM.release(mi)
            wrel(wi)
        wi = wload(lambda: dv["wT"][:, 8448:8512].rearrange("(c i) o -> i c o", i=128))
        b.wait("tensor", wstate["marks"][wi][0], wstate["marks"][wi][1])
        dtw = wslice(wi, KC, DTF)
        mi, mb = MM.acquire()
        for k in range(KC):
            t(lambda EE, mb=mb, k=k, dtw=dtw: EE["tensor"].matmul(
                mb[0:DTF, 0:twid], dtw[:, k, :], xsrc[:, k, tok0:tok0 + twid],
                start=(k == 0), stop=(k == KC - 1)), [("t", 1)] if k == KC - 1 else ())
        wrel(wi)
        b.wait("scalar", "t", b.mark("t"))
        s(lambda EE, mb=mb, twid=twid: EE["scalar"].activation(
            scrv[0:DTF, 0:twid], mb[0:DTF, 0:twid], AF.Exp, bias=dtbs[:, 0:1]))
        s(lambda EE, twid=twid: EE["scalar"].activation(
            dt_dst[:, dt_col0:dt_col0 + twid], scrv[0:DTF, 0:twid], AF.Ln, bias=1.0))
        MM.release(mi, "scalar")
        b.wait("vector", "s", b.mark("s"))

    cacc_state = {"i": 0, "guard": [0, 0], "gguard": 0, "vguard": 0}

    def conv_oc(oc, c0, dst, eng="vector"):
        pr = cacc_state["i"] % 2
        cacc_state["i"] += 1
        ca = cacc[:, pr, :]
        b.wait(eng, "s", cacc_state["guard"][pr])
        if eng == "gpsimd":
            b.wait("gpsimd", "g", cacc_state["gguard"])
        ee = eng
        def emit(fn, incs=()):
            b.op(ee, fn, incs)
            if ee == "gpsimd":
                b.wait("gpsimd", "g", b.mark("g"))
        emit(lambda EE, oc=oc, c0=c0, ca=ca: EE[ee].tensor_scalar(
            ca, biga[:, oc, c0:c0 + Q], convw[:, oc * 4:oc * 4 + 1], None, ALU.mult))
        for k in range(1, 4):
            emit(lambda EE, oc=oc, c0=c0, k=k, ca=ca: EE[ee].scalar_tensor_tensor(
                ca, biga[:, oc, c0 + k:c0 + k + Q], convw[:, oc * 4 + k:oc * 4 + k + 1],
                ca, ALU.mult, ALU.add))
        if ee == "gpsimd":
            cacc_state["gguard"] = b.mark("g")
            b.wait("scalar", "g", b.mark("g"))
        else:
            b.wait("scalar", "v", b.mark("v"))
        s(lambda EE, oc=oc, dst=dst, ca=ca: EE["scalar"].activation(
            dst, ca, AF.Silu, bias=convb[:, oc:oc + 1]), [("s", 1)])
        cacc_state["guard"][pr] = b.mark("s")

    def fm_to_tm(src, dsts):
        b.wait("tensor", "s", b.mark("s"))
        for qt in range(2):
            tbi, tbk = TRB.acquire()
            t(lambda EE, tbk=tbk, qt=qt, src=src: EE["tensor"].transpose(
                tbk[:, 0:128], src[:, qt * 128:(qt + 1) * 128], idb[:, :]), [("t", 1)])
            b.wait("vector", "t", b.mark("t"))
            v(lambda EE, tbk=tbk, d=dsts[qt]: EE["vector"].tensor_copy(d, tbk[:, 0:128]),
              [("pf", 1), ("v", 1)])
            TRB.release(tbi)

    def transpose_64wide(srcfn, dstf):
        for qt in range(2):
            mi, mb = MM.acquire()
            t(lambda EE, mb=mb, qt=qt, srcfn=srcfn: EE["tensor"].transpose(
                mb[0:128, 0:DTF], srcfn(qt), idf[0:64, 0:64]), [("t", 1)])
            b.wait("vector", "t", b.mark("t"))
            v(lambda EE, mb=mb, dstf=dstf, qt=qt: EE["vector"].tensor_copy(
                dstf[:, qt, :], mb[0:128, 0:DTF]), [("pf", 1)])
            MM.release(mi)

    convo_guard = [0, 0]

    def ssm_chunk(dt_src, dt_col0, emit_y, lc, first_chunk_S_zero):
        c0 = dt_col0 % 512
        sg_guard = [0, 0]
        g2_guard = [0, 0]
        b.wait("scalar", "t", b.mark("t"))
        chunk_t0 = b.mark("t")
        for og in range(0, OCX, 4):
            b.wait("scalar", "t", convo_guard[0])
            for j in range(4):
                conv_oc(og + j, c0, convo[:, j, :])
            b.wait("tensor", "s", b.mark("s"))
            for qt in range(2):
                tbi, tbk = TRB.acquire()
                for j in range(4):
                    t(lambda EE, tbk=tbk, j=j, qt=qt: EE["tensor"].transpose(
                        tbk[:, j * 128:(j + 1) * 128], convo[:, j, qt * 128:(qt + 1) * 128],
                        idb[:, :]), [("t", 1)] if j == 3 else ())
                b.wait("scalar", "t", b.mark("t"))
                s(lambda EE, tbk=tbk, qt=qt, og=og: EE["scalar"].copy(
                    xtm2[:, qt, og * 128:(og + 4) * 128], tbk[:, 0:512]))
                TRB.release(tbi, "scalar")
            convo_guard[0] = b.mark("t")
        conv_oc(OCB, c0, bfm[:, :])
        if emit_y:
            conv_oc(OCC, c0, cfm[:, :])
        fm_to_tm(bfm, [btm[:, 0, :], btm[:, 1, :]])

        b.wait("vector", "t", chunk_t0)
        b.wait("vector", "s", b.mark("s"))
        v(lambda EE: EE["vector"].tensor_scalar(csb[0:DTF, :], dt_src[:, dt_col0:dt_col0 + Q],
                                                acol[:, 0:1], None, ALU.mult))
        srcb, dstb = csb, csb2
        sh = 1
        while sh < Q:
            b.wait("vector", "v", b.mark("v"))
            v(lambda EE, a=dstb, c=srcb, sh=sh: EE["vector"].tensor_copy(a[0:DTF, 0:sh], c[0:DTF, 0:sh]))
            v(lambda EE, a=dstb, c=srcb, sh=sh: EE["vector"].tensor_tensor(
                a[0:DTF, sh:Q], c[0:DTF, sh:Q], c[0:DTF, 0:Q - sh], ALU.add))
            srcb, dstb = dstb, srcb
            sh *= 2
        b.wait("vector", "v", b.mark("v"))
        cs = srcb
        b.wait("tensor", "v", b.mark("v"))
        transpose_64wide(lambda qt: cs[0:DTF, qt * 128:(qt + 1) * 128], cst)
        transpose_64wide(lambda qt: dt_src[0:DTF, dt_col0 + qt * 128:dt_col0 + (qt + 1) * 128], dtt)
        v(lambda EE: EE["vector"].tensor_scalar(
            ncst.rearrange("p a h -> p (a h)"), cst.rearrange("p a h -> p (a h)"),
            -1.0, None, ALU.mult), [("v", 1)])
        if emit_y:
            b.wait("scalar", "v", b.mark("v"))
            s(lambda EE: EE["scalar"].activation(rfT[:, 0, :], cst[:, 0, :], AF.Exp))
            s(lambda EE: EE["scalar"].activation(rfT[:, 1, :], cst[:, 1, :], AF.Exp), [("s", 1)])
        fstate = {}

        def factor_chain():
            mi0, mb0 = MM.acquire()
            t(lambda EE, mb0=mb0, cs=cs: EE["tensor"].transpose(
                mb0[0:1, 0:DTF], cs[0:DTF, Q - 1:Q], idf[0:DTF, 0:DTF]), [("t", 1)])
            b.wait("vector", "t", b.mark("t"))
            v(lambda EE, mb0=mb0: EE["vector"].tensor_copy(t64a[0:1, 0:DTF], mb0[0:1, 0:DTF]),
              [("pf", 1), ("v", 1)])
            MM.release(mi0)
            b.wait("tensor", "v", b.mark("v"))
            mi, mb = MM.acquire()
            t(lambda EE, mb=mb: EE["tensor"].matmul(mb[:, 0:DTF], ones64[0:1, :], t64a[0:1, 0:DTF],
                                                    start=True, stop=True), [("t", 1)])
            b.wait("vector", "t", b.mark("t"))
            for qt in range(2):
                v(lambda EE, mb=mb, qt=qt: EE["vector"].tensor_tensor(
                    wsT[:, qt, :], mb[:, 0:DTF], cst[:, qt, :], ALU.subtract),
                  [("pf", 1), ("v", 1)] if qt == 1 else ())
            MM.release(mi)
            b.wait("scalar", "v", b.mark("v"))
            s(lambda EE: EE["scalar"].activation(wsT[:, 0, :], wsT[:, 0, :], AF.Exp))
            s(lambda EE: EE["scalar"].activation(wsT[:, 1, :], wsT[:, 1, :], AF.Exp), [("s", 1)])
            b.wait("vector", "s", b.mark("s"))
            v(lambda EE: EE["vector"].tensor_tensor(wsT[:, 0, :], wsT[:, 0, :], dtt[:, 0, :], ALU.mult))
            v(lambda EE: EE["vector"].tensor_tensor(wsT[:, 1, :], wsT[:, 1, :], dtt[:, 1, :], ALU.mult),
              [("v", 1)])
            s(lambda EE: EE["scalar"].activation(t64a[0:1, 0:DTF], t64a[0:1, 0:DTF], AF.Exp), [("s", 1)])
            b.wait("tensor", "s", b.mark("s"))
            cdi, cdb = SMH.acquire()
            fstate["cd"] = (cdi, cdb)
            t(lambda EE, cdb=cdb: EE["tensor"].matmul(cdb[:, 0:DTF], ones64[0:1, :],
                                                      t64a[0:1, 0:DTF], start=True, stop=True),
              [("t", 1)])
            fstate["cd_t"] = b.mark("t")


        def offs_chain():
            mi, mb = MM.acquire()
            t(lambda EE, mb=mb: EE["tensor"].matmul(mb[:, 0:DTF], ones64[0:1, :], cst[0:1, 1, :],
                                                    start=True, stop=True), [("t", 1)])
            b.wait("vector", "t", b.mark("t"))
            v(lambda EE, mb=mb: EE["vector"].tensor_tensor(woT_[:, :], mb[:, 0:DTF],
                                                           cst[:, 0, :], ALU.subtract))
            v(lambda EE, mb=mb: EE["vector"].tensor_tensor(roT_[:, :], cst[:, 1, :],
                                                           mb[:, 0:DTF], ALU.subtract),
              [("pf", 1), ("v", 1)])
            MM.release(mi)
            b.wait("scalar", "v", b.mark("v"))
            s(lambda EE: EE["scalar"].activation(woT_[:, :], woT_[:, :], AF.Exp))
            s(lambda EE: EE["scalar"].activation(roT_[:, :], roT_[:, :], AF.Exp), [("s", 1)])
            b.wait("vector", "s", b.mark("s"))
            v(lambda EE: EE["vector"].tensor_tensor(woT_[:, :], woT_[:, :], dtt[:, 0, :], ALU.mult),
              [("v", 1)])

        if not emit_y:
            factor_chain()
        if emit_y:
            for st in range(2):
                mi, mb = MM.acquire()
                q0 = 0 if st == 0 else 128
                t(lambda EE, mb=mb, st=st, q0=q0: EE["tensor"].matmul(
                    mb[:, 0:Q - q0], bfm[:, st * 128:(st + 1) * 128], cfm[:, q0:Q],
                    start=True, stop=True), [("t", 1)])
                b.wait("vector", "t", b.mark("t"))
                if st == 0:
                    v(lambda EE, mb=mb: EE["vector"].tensor_tensor(
                        cbt[:, 0, :], mb[:, 0:128], trilb[:, :], ALU.mult))
                    v(lambda EE, mb=mb: EE["vector"].tensor_copy(cbt[:, 1, :], mb[:, 128:256]),
                      [("pf", 1)])
                else:
                    v(lambda EE, mb=mb: EE["vector"].tensor_tensor(
                        cbt[:, 2, :], mb[:, 0:128], trilb[:, :], ALU.mult), [("pf", 1)])
                MM.release(mi)
            def xoff_chain():
                offs_chain()
                v(lambda EE: EE["vector"].tensor_tensor(
                    xscr[:, :].rearrange("p (h d) -> p h d", h=DTF),
                    xtm2[:, 0, :].rearrange("p (h d) -> p h d", h=DTF),
                    hbc(woT_[:, :], DTF), ALU.mult), [("v", 1)])
                fstate["xoff_v"] = b.mark("v")

            md_guard = {}
            mdv_marks = {}

            def stageA(i):
                qt, fb = divmod(i, 8)
                par = i % 2
                md = md8[:, par, :, :]
                b.wait("scalar", "t", md_guard.get(par, 0))
                for hh in range(8):
                    h = fb * 8 + hh
                    smi, smb = MM.acquire()
                    t(lambda EE, smb=smb, h=h, qt=qt, cs=cs: EE["tensor"].matmul(
                        smb[:, 0:128], idf[0:DTF, h:h + 1].broadcast_to([DTF, 128]),
                        cs[0:DTF, qt * 128:(qt + 1) * 128], start=True, stop=True),
                      [("t", 1)])
                    b.wait("scalar", "t", b.mark("t"))
                    uscr = sgt.rearrange("p a b -> p (a b)")
                    s(lambda EE, smb=smb, hh=hh, h=h, qt=qt, uscr=uscr: EE["scalar"].activation(
                        uscr[:, hh * 128:(hh + 1) * 128],
                        smb[:, 0:128], AF.Relu, bias=cst[:, qt, h:h + 1], scale=-1.0))
                    MM.release(smi, "scalar")
                    s(lambda EE, hh=hh, md=md, uscr=uscr: EE["scalar"].activation(
                        md[:, hh, :], uscr[:, hh * 128:(hh + 1) * 128],
                        AF.Exp, scale=-1.0), [("s", 1)])
                b.wait("vector", "s", b.mark("s"))
                for hh in range(8):
                    h = fb * 8 + hh
                    v(lambda EE, hh=hh, h=h, qt=qt, md=md: EE["vector"].scalar_tensor_tensor(
                        md[:, hh, :], md[:, hh, :], dtt[:, qt, h:h + 1],
                        cbt[:, 2 * qt, :], ALU.mult, ALU.mult), [("v", 1)] if hh == 7 else ())
                mdv_marks[i] = b.mark("v")

            def stageB(i):
                qt, fb = divmod(i, 8)
                par = i % 2
                md = md8[:, par, :, :]
                md_v = mdv_marks[i]
                ia, ba = MM.acquire()
                t(lambda EE, ba=ba, qt=qt, fb=fb: EE["tensor"].matmul(
                    ba[:, 0:512], cfm[:, qt * 128:(qt + 1) * 128],
                    sst[:, fb * 512:(fb + 1) * 512], start=True, stop=True), [("t", 1)])
                ib, bb = MM.acquire()
                b.wait("tensor", "v", md_v)
                for hh in range(8):
                    h = fb * 8 + hh
                    t(lambda EE, bb=bb, qt=qt, hh=hh, h=h, md=md: EE["tensor"].matmul(
                        bb[:, hh * 64:(hh + 1) * 64], md[:, hh, :],
                        xtm2[:, qt, h * 64:(h + 1) * 64],
                        start=(hh == 0), stop=(hh == 7)),
                      [("t", 1)] if hh == 7 else ())
                if qt == 1:
                    ic, bc_ = MM.acquire()
                    b.wait("tensor", "v", fstate["xoff_v"])
                    t(lambda EE, bc_=bc_, fb=fb: EE["tensor"].matmul(
                        bc_[:, 0:512], cbt[:, 1, :], xscr[:, fb * 512:(fb + 1) * 512],
                        start=True, stop=True), [("t", 1)])
                b.wait("vector", "t", b.mark("t"))
                ysl = ytm[:, fb * 512:(fb + 1) * 512].rearrange("p (h d) -> p h d", h=8)
                v(lambda EE, ba=ba, ysl=ysl, qt=qt, fb=fb: EE["vector"].tensor_tensor(
                    ysl, ba[:, 0:512].rearrange("p (h d) -> p h d", h=8),
                    hbc(rfT[:, qt, fb * 8:(fb + 1) * 8], 8), ALU.mult), [("pf", 1)])
                MM.release(ia)
                v(lambda EE, bb=bb, fb=fb: EE["vector"].tensor_tensor(
                    ytm[:, fb * 512:(fb + 1) * 512], ytm[:, fb * 512:(fb + 1) * 512],
                    bb[:, 0:512], ALU.add), [("pf", 1)])
                MM.release(ib)
                v(lambda EE, qt=qt, fb=fb: EE["vector"].tensor_tensor(
                    scrv[:, 0:512].rearrange("p (h d) -> p h d", h=8),
                    xtm2[:, qt, fb * 512:(fb + 1) * 512].rearrange("p (h d) -> p h d", h=8),
                    hbc(dbct[:, fb * 8:(fb + 1) * 8], 8), ALU.mult))
                v(lambda EE, fb=fb: EE["vector"].tensor_tensor(
                    ytm[:, fb * 512:(fb + 1) * 512], ytm[:, fb * 512:(fb + 1) * 512],
                    scrv[:, 0:512], ALU.add))
                if qt == 1:
                    v(lambda EE, bc_=bc_, fb=fb: EE["vector"].tensor_tensor(
                        scrv[:, 0:512].rearrange("p (h d) -> p h d", h=8),
                        bc_[:, 0:512].rearrange("p (h d) -> p h d", h=8),
                        hbc(roT_[:, fb * 8:(fb + 1) * 8], 8), ALU.mult), [("pf", 1)])
                    MM.release(ic)
                    v(lambda EE, fb=fb: EE["vector"].tensor_tensor(
                        ytm[:, fb * 512:(fb + 1) * 512], ytm[:, fb * 512:(fb + 1) * 512],
                        scrv[:, 0:512], ALU.add))
                md_guard[par] = b.mark("t")

            for qt in range(2):
                if qt == 0:
                    stageA(0)
                for fb in range(8):
                    i = qt * 8 + fb
                    if i + 1 < 16:
                        stageA(i + 1)
                    if i == 0:
                        xoff_chain()
                    if i == 1:
                        factor_chain()
                    stageB(i)
                b.wait("tensor", "v", b.mark("v"))
                for grp in range(4):
                    tbi, tbk = TRB.acquire()
                    for kk in range(8):
                        oc = grp * 8 + kk
                        t(lambda EE, tbk=tbk, kk=kk, oc=oc: EE["tensor"].transpose(
                            tbk[:, kk * 128:(kk + 1) * 128], ytm[:, oc * 128:(oc + 1) * 128],
                            idb[:, :]), [("t", 1)] if kk == 7 else ())
                    b.wait("vector", "t", b.mark("t"))
                    col = qt * 128
                    v(lambda EE, tbk=tbk, grp=grp, col=col: EE["vector"].tensor_copy(
                        ypair[:, grp * 8:(grp + 1) * 8, col:col + 128],
                        tbk[:, 0:1024].rearrange("p (a b2) -> p a b2", a=8)), [("pf", 1)])
                    TRB.release(tbi)

        if emit_y:
            tok0 = dt_col0
            ss_i, ss_b = SML.acquire()
            for oc in range(32):
                o0 = oc * 128
                wi = wload(lambda o0=o0: dv["wT"][:, o0:o0 + 128].rearrange("(c i) o -> i c o", i=128))
                b.wait("tensor", wstate["marks"][wi][0], wstate["marks"][wi][1])
                ws = wslice(wi, KC, 128)
                mi, mb = MM.acquire()
                for k in range(KC):
                    t(lambda EE, mb=mb, ws=ws, k=k: EE["tensor"].matmul(
                        mb[:, 0:Q], ws[:, k, 0:128], xn[:, k, tok0:tok0 + Q],
                        start=(k == 0), stop=(k == KC - 1)), [("t", 1)] if k == KC - 1 else ())
                wrel(wi)
                pr = oc % 2
                b.wait("scalar", "t", b.mark("t"))
                b.wait("scalar", "v", sg_guard[pr])
                s(lambda EE, mb=mb, pr=pr: EE["scalar"].activation(
                    sgt[:, pr, 0:Q], mb[:, 0:Q], AF.Silu))
                MM.release(mi, "scalar")
                b.wait("vector", "s", b.mark("s"))
                b.wait("vector", "t", g2_guard[pr])
                v(lambda EE, oc=oc, pr=pr: EE["vector"].tensor_tensor(
                    ypair[:, oc, :], ypair[:, oc, :], sgt[:, pr, 0:Q], ALU.mult))
                v(lambda EE, oc=oc, pr=pr: EE["vector"].tensor_tensor(
                    g2t[:, pr, 0:Q], ypair[:, oc, :], ypair[:, oc, :], ALU.mult), [("v", 1)])
                sg_guard[pr] = b.mark("v")
                b.wait("tensor", "v", b.mark("v"))
                t(lambda EE, ss_b=ss_b, oc=oc, pr=pr: EE["tensor"].matmul(
                    ss_b[0:1, 0:Q], onescol[:, 0:1], g2t[:, pr, 0:Q],
                    start=(oc == 0), stop=(oc == 31)), [("t", 1)] if oc == 31 else ())
                g2_guard[pr] = b.mark("t")
            b.wait("vector", "t", b.mark("t"))
            v(lambda EE, ss_b=ss_b: EE["vector"].tensor_scalar(
                rr[0:1, 0:Q], ss_b[0:1, 0:Q], 1.0 / I, EPS, ALU.mult, ALU.add), [("v", 1)])
            SML.release(ss_i)
            b.wait("scalar", "v", b.mark("v"))
            s(lambda EE: EE["scalar"].activation(rr[0:1, 0:Q], rr[0:1, 0:Q], AF.Sqrt), [("s", 1)])
            b.wait("vector", "s", b.mark("s"))
            vred(lambda EE: EE["vector"].reciprocal(rr[0:1, 0:Q], rr[0:1, 0:Q]))
            b.wait("tensor", "v", b.mark("v"))
            rs_i, rs_b = SML.acquire()
            t(lambda EE, rs_b=rs_b: EE["tensor"].matmul(rs_b[:, 0:Q], ones64[0:1, :],
                                                        rr[0:1, 0:Q], start=True, stop=True),
              [("t", 1)])
            b.wait("vector", "t", b.mark("t"))
            for oc in range(OCX):
                v(lambda EE, oc=oc, rs_b=rs_b: EE["vector"].scalar_tensor_tensor(
                    ypair[:, oc, :], ypair[:, oc, :], nwpc[:, oc:oc + 1], rs_b[:, 0:Q],
                    ALU.mult, ALU.mult), [("v", 1)] if oc == OCX - 1 else ())
            SML.release(rs_i)
            gn_v = b.mark("v")
            b.wait("tensor", "v", gn_v)
            for oc in range(KC):
                o0 = oc * 128
                wis = [wload(lambda o0=o0, hk=hk: dv["woT"][hk * 2048:(hk + 1) * 2048, o0:o0 + 128]
                             .rearrange("(c i) o -> i c o", i=128)) for hk in range(2)]
                mi, mb = MM.acquire()
                for hk in range(2):
                    b.wait("tensor", wstate["marks"][wis[hk]][0], wstate["marks"][wis[hk]][1])
                    ws = wslice(wis[hk], 16, 128)
                    for kk in range(16):
                        k = hk * 16 + kk
                        t(lambda EE, mb=mb, ws=ws, kk=kk, k=k: EE["tensor"].matmul(
                            mb[:, 0:Q], ws[:, kk, 0:128], ypair[:, k, :],
                            start=(k == 0), stop=(k == 31)), [("t", 1)] if k == 31 else ())
                    wrel(wis[hk])
                b.wait("vector", "t", b.mark("t"))
                v(lambda EE, mb=mb, oc=oc, tok0=tok0: EE["vector"].tensor_copy(
                    outacc[:, oc, tok0:tok0 + Q], mb[:, 0:Q]), [("pf", 1)])
                MM.release(mi)

        for pp in range(4):
            b.wait("vector", "t", b.mark("t"))
            for half in range(2):
                for sub in range(2):
                    fb = pp * 2 + half
                    v(lambda EE, sub=sub, fb=fb, half=half: EE["vector"].tensor_tensor(
                        xscr[:, (half * 2 + sub) * 1024:(half * 2 + sub) * 1024 + 512]
                        .rearrange("p (h d) -> p h d", h=8),
                        xtm2[:, sub, fb * 512:(fb + 1) * 512].rearrange("p (h d) -> p h d", h=8),
                        hbc(wsT[:, sub, fb * 8:(fb + 1) * 8], 8), ALU.mult),
                      [("v", 1)] if (half, sub) == (1, 1) else ())
            stv = b.mark("v")
            bks = []
            for half in range(2):
                mi, mb = MM.acquire()
                bks.append((mi, mb))
                if half == 0:
                    b.wait("tensor", "v", stv)
                for sub in range(2):
                    t(lambda EE, mb=mb, half=half, sub=sub: EE["tensor"].matmul(
                        mb[:, 0:512], btm[:, sub, :],
                        xscr[:, (half * 2 + sub) * 1024:(half * 2 + sub) * 1024 + 512],
                        start=(sub == 0), stop=(sub == 1)),
                      [("t", 1)] if sub == 1 else ())
            b.wait("vector", "t", b.mark("t"))
            if pp == 0:
                b.wait("vector", "t", fstate["cd_t"])
            for half in range(2):
                fb = pp * 2 + half
                mi, mb = bks[half]
                ssl = sst[:, fb * 512:(fb + 1) * 512].rearrange("p (h d) -> p h d", h=8)
                if first_chunk_S_zero:
                    v(lambda EE, mb=mb, fb=fb: EE["vector"].tensor_copy(
                        sst[:, fb * 512:(fb + 1) * 512], mb[:, 0:512]), [("pf", 1)])
                else:
                    v(lambda EE, ssl=ssl, fb=fb, cdb=fstate["cd"][1]: EE["vector"].tensor_tensor(
                        ssl, ssl, hbc(cdb[:, fb * 8:(fb + 1) * 8], 8), ALU.mult))
                    v(lambda EE, mb=mb, fb=fb: EE["vector"].tensor_tensor(
                        sst[:, fb * 512:(fb + 1) * 512], sst[:, fb * 512:(fb + 1) * 512],
                        mb[:, 0:512], ALU.add), [("pf", 1)])
                MM.release(mi)
        SMH.release(fstate["cd"][0])

    inproj_xbc(xnh, 0, HALO, dthB, 0)
    for ti in range(8):
        ln_tile(ti, dv["xm"][ti * 128:(ti + 1) * 128, :], xn, ti * 128)
    v(lambda EE: EE["vector"].memset(biga[:, :, 0:3], 0.0), [("v", 1)])
    ssm_chunk(dthB, 0, False, 0, True)
    v(lambda EE: EE["vector"].tensor_scalar(sst[:, :], sst[:, :], hscs[:, 0:1], None, ALU.mult))
    v(lambda EE: EE["vector"].tensor_scalar(
        lead3[:, :, :], biga[:, :, HALO:HALO + 3], hscs[:, 0:1], None, ALU.mult),
      [("v", 1)])

    inproj_xbc(xn, 0, 512, dtmB, 0)
    v(lambda EE: EE["vector"].tensor_copy(biga[:, :, 0:3], lead3[:, :, :]), [("v", 1)])
    ssm_chunk(dtmB, 0, True, 0, False)
    ssm_chunk(dtmB, 256, True, 1, False)
    v(lambda EE: EE["vector"].tensor_copy(lead3[:, :, :], biga[:, :, 512:515]),
      [("v", 1)])

    inproj_xbc(xn, 512, 512, dtmB, 512)
    v(lambda EE: EE["vector"].tensor_copy(biga[:, :, 0:3], lead3[:, :, :]), [("v", 1)])
    ssm_chunk(dtmB, 512, True, 0, False)
    ssm_chunk(dtmB, 768, True, 1, False)

    for tt in range(8):
        smi, smb = SML.acquire()
        for k in range(KC):
            t(lambda EE, smb=smb, k=k, tt=tt: EE["tensor"].matmul(
                smb[:, 0:E], xn[:, k, tt * 128:(tt + 1) * 128], rw8[:, k, :],
                start=(k == 0), stop=(k == KC - 1)), [("t", 1)] if k == KC - 1 else ())
        b.wait("vector", "t", b.mark("t"))
        v(lambda EE, smb=smb: EE["vector"].tensor_copy(lgt[:, 0:8], smb[:, 0:E]), [("pf", 1)])
        SML.release(smi)
        vred(lambda EE: EE["vector"].reduce_max(m12[:, 1:2], lgt[:, 0:8], axis=AXX, negate=True))
        b.wait("scalar", "v", b.mark("v"))
        s(lambda EE: EE["scalar"].activation(lgt[:, 8:16], lgt[:, 0:8], AF.Exp,
                                             bias=m12[:, 1:2]), [("s", 1)])
        b.wait("vector", "s", b.mark("s"))
        vred(lambda EE: EE["vector"].reduce_sum(m12[:, 2:3], lgt[:, 8:16], axis=AXX))
        vred(lambda EE: EE["vector"].reciprocal(m12[:, 2:3], m12[:, 2:3]))
        v(lambda EE: EE["vector"].tensor_scalar(lgt[:, 8:16], lgt[:, 8:16], m12[:, 2:3],
                                                None, ALU.mult))
        vred(lambda EE: EE["vector"].reduce_max(m12[:, 0:1], lgt[:, 8:16], axis=AXX))
        v(lambda EE: EE["vector"].tensor_scalar(lgt[:, 0:8], lgt[:, 8:16], m12[:, 0:1],
                                                None, ALU.is_equal))
        b.wait("vector", "v", b.mark("v"))
        v(lambda EE: EE["vector"].scalar_tensor_tensor(scrv[:, 0:8], lgt[:, 0:8], -1e30,
                                                       lgt[:, 8:16], ALU.mult, ALU.add))
        vred(lambda EE: EE["vector"].reduce_max(m12[:, 1:2], scrv[:, 0:8], axis=AXX))
        v(lambda EE: EE["vector"].tensor_scalar(scrv[:, 8:16], scrv[:, 0:8], m12[:, 1:2],
                                                None, ALU.is_equal))
        b.wait("vector", "v", b.mark("v"))
        v(lambda EE: EE["vector"].tensor_tensor(lgt[:, 0:8], lgt[:, 0:8], scrv[:, 8:16], ALU.add))
        b.wait("vector", "v", b.mark("v"))
        v(lambda EE: EE["vector"].tensor_tensor(lgt[:, 0:8], lgt[:, 0:8], lgt[:, 8:16], ALU.mult))
        v(lambda EE: EE["vector"].tensor_tensor(m12[:, 3:4], m12[:, 0:1], m12[:, 1:2], ALU.add))
        vred(lambda EE: EE["vector"].reciprocal(m12[:, 3:4], m12[:, 3:4]))
        b.wait("vector", "v", b.mark("v"))
        v(lambda EE, tt=tt: EE["vector"].tensor_scalar(comb[:, tt * 8:(tt + 1) * 8], lgt[:, 0:8],
                                                       m12[:, 3:4], None, ALU.mult), [("v", 1)])
        b.wait("tensor", "v", b.mark("v"))
        mi, mb = MM.acquire()
        t(lambda EE, mb=mb, tt=tt: EE["tensor"].transpose(
            mb[0:E, 0:128], comb[:, tt * 8:(tt + 1) * 8], idf[:, :]), [("t", 1)])
        b.wait("vector", "t", b.mark("t"))
        v(lambda EE, mb=mb, tt=tt: EE["vector"].tensor_copy(
            combT[0:E, tt * 128:(tt + 1) * 128], mb[0:E, 0:128]), [("pf", 1)])
        MM.release(mi)

    amoe_read_t = 0
    for ee in range(E):
        cb_is = []
        for th in range(2):
            smi, smb = (SML if th == 0 else SMH).acquire()
            t(lambda EE, smb=smb, ee=ee, th=th: EE["tensor"].matmul(
                smb[:, 0:512], idb[0:E, ee:ee + 1].broadcast_to([E, 128]),
                combT[0:E, th * 512:(th + 1) * 512], start=True, stop=True), [("t", 1)])
            cb_is.append((smi, smb))
        cb_t = b.mark("t")
        b.wait("vector", "t", cb_t)
        sgj_guard = [0, 0]
        for j in range(JC):
            par = j % 2
            for role in range(2):
                oc = j if role == 0 else JC + j
                o0 = oc * 128
                wi = wload(lambda o0=o0, ee=ee: dv["gupT"][ee, :, o0:o0 + 128]
                           .rearrange("(c i) o -> i c o", i=128))
                b.wait("tensor", wstate["marks"][wi][0], wstate["marks"][wi][1])
                ws = wslice(wi, KC, 128)
                for th in range(2):
                    mi, mb = MM.acquire()
                    for k in range(KC):
                        incs = [("t", 1)] if k == KC - 1 else []
                        t(lambda EE, mb=mb, ws=ws, k=k, th=th: EE["tensor"].matmul(
                            mb[:, 0:512], ws[:, k, 0:128],
                            xn[:, k, th * 512:(th + 1) * 512],
                            start=(k == 0), stop=(k == KC - 1)), incs)
                    if role == 0:
                        b.wait("scalar", "t", b.mark("t"))
                        if th == 0:
                            b.wait("scalar", "v", sgj_guard[par])
                        s(lambda EE, mb=mb, par=par, th=th: EE["scalar"].activation(
                            sgj[:, par, th, :], mb[:, 0:512], AF.Silu))
                        MM.release(mi, "scalar")
                        b.wait("vector", "s", b.mark("s"))
                        v(lambda EE, par=par, th=th, cb=cb_is[th][1]: EE["vector"].tensor_tensor(
                            sgj[:, par, th, :], sgj[:, par, th, :], cb[:, 0:512],
                            ALU.mult), [("v", 1)])
                    else:
                        b.wait("vector", "t", b.mark("t"))
                        b.wait("vector", "t", amoe_read_t)
                        v(lambda EE, mb=mb, j2=j, th=th, par=par: EE["vector"].tensor_tensor(
                            amoe[:, j2, th * 512:(th + 1) * 512], mb[:, 0:512],
                            sgj[:, par, th, :], ALU.mult), [("pf", 1), ("v", 1)])
                        MM.release(mi)
                if role == 1:
                    sgj_guard[par] = b.mark("v")
                wrel(wi)
        SML.release(cb_is[0][0])
        SMH.release(cb_is[1][0])
        a_v = b.mark("v")
        b.wait("tensor", "v", a_v)
        for oc in range(KC):
            o0 = oc * 128
            wi = wload(lambda o0=o0, ee=ee: dv["downT"][ee, :, o0:o0 + 128]
                       .rearrange("(c i) o -> i c o", i=128))
            b.wait("tensor", wstate["marks"][wi][0], wstate["marks"][wi][1])
            if True:
                ws = wslice(wi, JC, 128)
                for th in range(2):
                    mi, mb = MM.acquire()
                    for k in range(JC):
                        incs = [("t", 1)] if k == JC - 1 else []
                        t(lambda EE, mb=mb, ws=ws, k=k, th=th: EE["tensor"].matmul(
                            mb[:, 0:512], ws[:, k, 0:128],
                            amoe[:, k, th * 512:(th + 1) * 512],
                            start=(k == 0), stop=(k == JC - 1)), incs)
                    b.wait("vector", "t", b.mark("t"))
                    v(lambda EE, mb=mb, oc=oc, th=th: EE["vector"].tensor_tensor(
                        outacc[:, oc, th * 512:(th + 1) * 512],
                        outacc[:, oc, th * 512:(th + 1) * 512], mb[:, 0:512], ALU.add),
                      [("pf", 1)])
                    MM.release(mi)
                wrel(wi)
        amoe_read_t = b.mark("t")

    b.wait("tensor", "v", b.mark("v"))
    moe_v = b.mark("v")
    x5_free = [moe_v, moe_v]
    for tt in range(8):
        hb = tt % 2
        b.wait("sync", "v", x5_free[hb])
        xs_ = f"x{hb}"
        sy(lambda EE, hb=hb, tt=tt: EE["sync"].dma_start(
            out=xld5[hb][:, 0:2048], in_=dv["xm"][tt * 128:(tt + 1) * 128, :]), [(xs_, 16)])
        xmark = b.mark(xs_)
        for grp in range(2):
            tbi, tbk = TRB.acquire()
            for kk in range(8):
                oc = grp * 8 + kk
                t(lambda EE, tbk=tbk, kk=kk, oc=oc, tt=tt: EE["tensor"].transpose(
                    tbk[:, kk * 128:(kk + 1) * 128], outacc[:, oc, tt * 128:(tt + 1) * 128],
                    idb[:, :]), [("t", 1)] if kk == 7 else ())
            b.wait("vector", "t", b.mark("t"))
            if grp == 0:
                b.wait("vector", xs_, xmark)
            v(lambda EE, tbk=tbk, grp=grp, hb=hb: EE["vector"].tensor_tensor(
                xout5[hb][:, grp * 1024:(grp + 1) * 1024], xld5[hb][:, grp * 1024:(grp + 1) * 1024],
                tbk[:, 0:1024], ALU.add), [("pf", 1), ("v", 1)])
            TRB.release(tbi)
        x5_free[hb] = b.mark("v")
        b.wait("sync", "v", b.mark("v"))
        sy(lambda EE, hb=hb, tt=tt: EE["sync"].dma_start(
            out=dv["out"][tt * 128:(tt + 1) * 128, :], in_=xout5[hb][:, 0:2048]), [("o", 16)])
    b.wait("sync", "o", b.mark("o"))

    waited = {}
    for eng in ENGINES:
        for fn, incs in b.ops[eng]:
            if isinstance(fn, tuple) and fn[0] == "wait":
                waited.setdefault(fn[1], set()).add(fn[2])
    cnt2 = {}
    keep = {}
    newval = {}
    cum = {}
    for eng in ENGINES:
        for fn, incs in b.ops[eng]:
            if isinstance(fn, tuple) and fn[0] == "wait":
                continue
            for sname, n in incs:
                pre = cnt2.get(sname, 0)
                post = pre + n
                cnt2[sname] = post
                ws = [w for w in waited.get(sname, ()) if pre < w <= post]
                if ws or n == 16:
                    keep[(sname, post)] = True
                    cum[sname] = cum.get(sname, 0) + n
                    for w in ws:
                        newval[(sname, w)] = cum[sname]
    cnt3 = {}
    newops = {}
    for eng in ENGINES:
        lst = []
        for fn, incs in b.ops[eng]:
            if isinstance(fn, tuple) and fn[0] == "wait":
                lst.append(((fn[0], fn[1], newval[(fn[1], fn[2])]), ()))
                continue
            ni = []
            for sname, n in incs:
                pre = cnt3.get(sname, 0)
                post = pre + n
                cnt3[sname] = post
                if keep.get((sname, post)):
                    ni.append((sname, n))
            lst.append((fn, tuple(ni)))
        newops[eng] = lst
    b.ops = newops

    with nc.Block() as block:
        def replay(eng_name):
            def f(EO):
                EE = {eng_name: EO}
                for fn, incs in b.ops[eng_name]:
                    if isinstance(fn, tuple) and fn[0] == "wait":
                        EO.wait_ge(sems[fn[1]], fn[2])
                        continue
                    inst = fn(EE)
                    for sname, n in incs:
                        inst.then_inc(sems[sname], n)
            return f

        block.tensor(replay("tensor"))
        block.vector(replay("vector"))
        block.scalar(replay("scalar"))
        block.gpsimd(replay("gpsimd"))
        block.sync(replay("sync"))

    es.close()
    return nc


_NC_CACHE = {}


def kernel(**inputs):
    x = np.asarray(inputs["L0_attn_hidden_states"], dtype=np.float32)
    wT = np.ascontiguousarray(np.asarray(inputs["L0_attn_in_proj_w"], np.float32).T)
    woT = np.ascontiguousarray(np.asarray(inputs["L0_attn_out_proj_w"], np.float32).T)
    gupT = np.ascontiguousarray(np.asarray(inputs["L0_ffn_gate_up_proj"], np.float32).transpose(0, 2, 1))
    downT = np.ascontiguousarray(np.asarray(inputs["L0_ffn_down_proj"], np.float32).transpose(0, 2, 1))
    routT = np.ascontiguousarray(np.asarray(inputs["L0_ffn_router_w"], np.float32).T)
    conv_w = np.asarray(inputs["L0_attn_conv_w"], np.float32)
    conv_b = np.asarray(inputs["L0_attn_conv_b"], np.float32)
    dt_bias = np.asarray(inputs["L0_attn_dt_bias"], np.float32)
    A = np.asarray(inputs["L0_attn_A"], np.float32)
    D = np.asarray(inputs["L0_attn_D"], np.float32)
    nw = np.asarray(inputs["L0_attn_norm_w"], np.float32)

    convw_pc = np.ascontiguousarray(conv_w.reshape(NOC, 128, 4).transpose(1, 0, 2).reshape(128, NOC * 4))
    convb_pc = np.ascontiguousarray(conv_b.reshape(NOC, 128).T)
    nw_pc = np.ascontiguousarray(nw.reshape(OCX, 128).T)

    if "nc" not in _NC_CACHE:
        _NC_CACHE["nc"] = build_nc()
    nc = _NC_CACHE["nc"]

    in_maps = []
    for c in range(8):
        sq, qq = c // 4, c % 4
        xm = np.ascontiguousarray(x[sq, qq * TM:(qq + 1) * TM])
        if qq == 0:
            xh = np.ascontiguousarray(x[sq, 0:HALO])
            hs = 0.0
        else:
            xh = np.ascontiguousarray(x[sq, qq * TM - HALO:qq * TM])
            hs = 1.0
        in_maps.append({
            "xm": xm, "xh": xh, "hsc": np.full((128, 1), hs, np.float32),
            "wT": wT, "woT": woT, "gupT": gupT, "downT": downT, "routT": routT,
            "convw": convw_pc, "convb": convb_pc, "nwpc": nw_pc,
            "acol": A.reshape(DTF, 1), "dtb": dt_bias.reshape(DTF, 1),
            "drow": D.reshape(1, DTF),
        })
    res = run_bass_kernel_spmd(nc, in_maps, core_ids=list(range(8)))
    out = np.zeros((2, 4096, HID), np.float32)
    for c in range(8):
        sq, qq = c // 4, c % 4
        out[sq, qq * TM:(qq + 1) * TM] = res.results[c]["out"]
    return out
